# version 14
# baseline (speedup 1.0000x reference)
import contextlib

import numpy as np
import concourse.bass as bass
import concourse.mybir as mybir
from concourse.bass_utils import run_bass_kernel_spmd

F32 = mybir.dt.float32
BF16 = mybir.dt.bfloat16
AF = mybir.ActivationFunctionType
ALU = mybir.AluOpType
AX = mybir.AxisListType

EPS_RMS = float(np.finfo(np.float32).eps)
EPS_LN = 1e-5

ENGINES = ("sync", "gpsimd", "tensor", "scalar", "vector")
DMA_QUEUES = ("sync", "gpsimd")
N_DMA_SLOTS = 8


class Prog:

    def __init__(self, nc, ctx):
        self.nc = nc
        self.ops = {e: [] for e in ENGINES}
        self.regions = {}
        self.psem = {e: ctx.enter_context(nc.semaphore(f"prog_{e}"))
                     for e in ("tensor", "scalar", "vector", "gpsimd")}
        self.dsem = {q: [ctx.enter_context(nc.semaphore(f"dma_{q}_{i}"))
                         for i in range(N_DMA_SLOTS)] for q in DMA_QUEUES}
        self.dma_count = {q: 0 for q in DMA_QUEUES}
        self.pending_barrier = {e: [] for e in ENGINES}

    def _region(self, key):
        r = self.regions.get(key)
        if r is None:
            r = {"w": None, "r": []}
            self.regions[key] = r
        return r

    def _record(self, engine, emit, reads, writes, is_dma, dma_slot=None,
                dma_target=None):
        op = {
            "engine": engine, "emit": emit, "idx": len(self.ops[engine]),
            "flag": False, "value": None, "deps": [],
            "is_dma": is_dma, "dma_slot": dma_slot, "dma_target": dma_target,
        }
        deps = list(self.pending_barrier[engine])
        self.pending_barrier[engine] = []
        for key in reads:
            r = self._region(key)
            if r["w"] is not None:
                deps.append(r["w"])
            r["r"].append(op)
        for key in writes:
            r = self._region(key)
            deps.extend(r["r"])
            if r["w"] is not None:
                deps.append(r["w"])
            r["w"] = op
            r["r"] = []
        seen = set()
        for d in deps:
            if d is op or id(d) in seen:
                continue
            seen.add(id(d))
            if d["engine"] == engine and not d["is_dma"] and not is_dma \
                    and engine == "tensor" and not d.get("bar"):
                continue
            if not d["is_dma"]:
                d["flag"] = True
            op["deps"].append(d)
        self.ops[engine].append(op)
        return op

    def op(self, engine, emit, reads=(), writes=()):
        return self._record(engine, emit, reads, writes, False)

    def dma(self, queue, emit, reads=(), writes=()):
        j = self.dma_count[queue]
        self.dma_count[queue] = j + 1
        slot = j % N_DMA_SLOTS
        target = 16 * (j // N_DMA_SLOTS + 1)
        op = self._record(queue, emit, reads, writes, True, slot, target)
        if j >= N_DMA_SLOTS:
            op["deps"].append({
                "engine": queue, "is_dma": True, "dma_slot": slot,
                "dma_target": target - 16, "flag": False, "_sentinel": True,
            })
        return op

    def barrier(self):
        deps = []
        for e in ("tensor", "scalar", "vector"):
            if self.ops[e]:
                last = self.ops[e][-1]
                last["flag"] = True
                deps.append(last)
        for q in DMA_QUEUES:
            slot_max = {}
            for o in self.ops[q]:
                if o["is_dma"]:
                    slot_max[o["dma_slot"]] = o
            deps.extend(slot_max.values())
        for d in deps:
            d["bar"] = True
        for e in ENGINES:
            self.pending_barrier[e] = list(deps)

    def finalize_values(self):
        for e in ENGINES:
            v = 0
            for op in self.ops[e]:
                if op["flag"] and not op["is_dma"]:
                    v += 1
                    op["value"] = v

    def emit_engine(self, engine, e):
        watermark = {}
        for op in self.ops[engine]:
            for d in op["deps"]:
                if d["is_dma"]:
                    sem = self.dsem[d["engine"]][d["dma_slot"]]
                    tgt = d["dma_target"]
                else:
                    sem = self.psem[d["engine"]]
                    tgt = d["value"]
                    assert tgt is not None
                k = id(sem)
                if watermark.get(k, -1) >= tgt:
                    continue
                watermark[k] = tgt
                e.wait_ge(sem, tgt)
            inst = op["emit"](e)
            if op["is_dma"]:
                inst.then_inc(self.dsem[engine][op["dma_slot"]], 16)
            elif op["flag"]:
                inst.then_inc(self.psem[engine], 1)

    def final_wait(self, engine, e, targets):
        for q in DMA_QUEUES:
            slot_max = {}
            for o in self.ops[q]:
                if o["is_dma"]:
                    slot_max[o["dma_slot"]] = o["dma_target"]
            for slot, tgt in sorted(slot_max.items()):
                e.wait_ge(self.dsem[q][slot], tgt)


def build_megakernel(T_main=4096, T_pref=4096, NB=1024, NT=256):
    assert T_main % NT == 0 and T_pref % NT == 0
    assert T_main % NB == 0 and NB % 128 == 0
    NSUB = 512 if NB >= 512 else NB
    n_ns = NB // NSUB
    D, F = 1024, 4096
    NDT = D // 128
    NFC = F // 128
    NEXP = 9
    n_nb = T_main // NB
    nt_pref = T_pref // NT
    nt_main = T_main // NT
    NPT = NT // 128
    NG = NT // 128

    nc = bass.Bass("TRN2", target_bir_lowering=False)

    x_main = nc.dram_tensor("x_main", [T_main, D], F32, kind="ExternalInput").ap()
    x_pref = nc.dram_tensor("x_pref", [max(T_pref, 1), D], F32, kind="ExternalInput").ap()
    mask_in = nc.dram_tensor("mask", [128, 1], F32, kind="ExternalInput").ap()
    WrT = nc.dram_tensor("WrT", [D, D], F32, kind="ExternalInput").ap()
    WkT = nc.dram_tensor("WkT", [D, D], F32, kind="ExternalInput").ap()
    WvT = nc.dram_tensor("WvT", [D, D], F32, kind="ExternalInput").ap()
    WwT = nc.dram_tensor("WwT", [D, D], F32, kind="ExternalInput").ap()
    WoT = nc.dram_tensor("WoT", [D, D], F32, kind="ExternalInput").ap()
    gWT = nc.dram_tensor("gWT", [D, 8], F32, kind="ExternalInput").ap()
    Wst1 = nc.dram_tensor("Wst1", [NEXP * NFC, 128, 2048], F32, kind="ExternalInput").ap()
    Wst2 = nc.dram_tensor("Wst2", [NEXP * NDT, 128, 4096], F32, kind="ExternalInput").ap()
    kbE = nc.dram_tensor("kbE", [16, D], BF16, kind="ExternalInput").ap()
    idn_in = nc.dram_tensor("idn", [128, 128], F32, kind="ExternalInput").ap()
    idnb_in = nc.dram_tensor("idnb", [128, 128], BF16, kind="ExternalInput").ap()
    ones_col_in = nc.dram_tensor("ones_col", [128, 1], F32, kind="ExternalInput").ap()
    ones_colb_in = nc.dram_tensor("ones_colb", [128, 1], BF16, kind="ExternalInput").ap()
    ones_row_in = nc.dram_tensor("ones_row", [1, 128], F32, kind="ExternalInput").ap()
    out_t = nc.dram_tensor("out", [T_main, D], F32, kind="ExternalOutput").ap()

    hscr = nc.dram_tensor("hscr", [D, T_main], F32, kind="Internal").ap()
    hscr_bf = nc.dram_tensor("hscr_bf", [D, T_main], BF16, kind="Internal").ap()
    gscr = nc.dram_tensor("gscr", [8, T_main], F32, kind="Internal").ap()
    wbs1 = nc.dram_tensor("wbs1", [NEXP * NFC, 128, 2048], BF16, kind="Internal").ap()
    wbs2 = nc.dram_tensor("wbs2", [NEXP * NDT, 128, 4096], BF16, kind="Internal").ap()

    out_stores = []

    with contextlib.ExitStack() as ctx:
        P = Prog(nc, ctx)

        def sb(name, shape, dtype, stack=None):
            return (stack or ctx).enter_context(nc.sbuf_tensor(name, shape, dtype))

        def ps(name, stack=None):
            return (stack or ctx).enter_context(nc.psum_tensor(name, [128, 512], F32))

        idn = sb("idn_s", [128, 128], F32)
        idnb = sb("idnb_s", [128, 128], BF16)
        ones_col = sb("ones_col_s", [128, 1], F32)
        ones_colb = sb("ones_colb_s", [128, 1], BF16)
        ones_row = sb("ones_row_s", [1, 128], F32)
        mask_s = sb("mask_s", [128, 1], F32)
        gwts = sb("gwts", [128, 64], F32)
        minit = sb("minit", [128, NDT], F32)
        consts = sb("consts", [128, 2], F32)

        P.op("vector", lambda e: e.memset(consts[:, 0:1], EPS_RMS), writes=["consts"])
        P.op("vector", lambda e: e.memset(consts[:, 1:2], EPS_LN), writes=["consts"])
        P.dma("sync", lambda e: e.dma_start(out=idn[:], in_=idn_in), writes=["idn"])
        P.dma("sync", lambda e: e.dma_start(out=idnb[:], in_=idnb_in), writes=["idnb"])
        P.dma("sync", lambda e: e.dma_start(out=ones_col[:], in_=ones_col_in), writes=["ones_col"])
        P.dma("sync", lambda e: e.dma_start(out=ones_colb[:], in_=ones_colb_in), writes=["ones_colb"])
        P.dma("sync", lambda e: e.dma_start(out=ones_row[:], in_=ones_row_in), writes=["ones_row"])
        P.dma("sync", lambda e: e.dma_start(out=mask_s[:], in_=mask_in), writes=["mask"])
        P.dma("sync", lambda e: e.dma_start(
            out=gwts[:].rearrange("p (kt e) -> p kt e", kt=8),
            in_=gWT.rearrange("(kt kp) e -> kp kt e", kp=128)),
            writes=["gwts"])

        with contextlib.ExitStack() as actx:
            wv_bf = sb("wv_bf", [128, 8 * D], BF16, actx)
            ww_bf = sb("ww_bf", [128, 8 * D], BF16, actx)
            wr_bf = sb("wr_bf", [128, 8 * D], BF16, actx)
            wo_bf = sb("wo_bf", [128, 8 * D], BF16, actx)
            wkts = sb("wkts", [128, 8 * 16], BF16, actx)
            kbE_s = sb("kbE_s", [16, D], BF16, actx)
            xs = [sb(f"xs{i}", [128, NPT * D], F32, actx) for i in range(3)]
            xs_bf = [sb(f"xsb{i}", [128, NPT * D], BF16, actx) for i in range(2)]
            xT = sb("xT", [128, NDT * NT], F32, actx)
            xTb = [sb(f"xTb{i}", [128, NDT * NT], BF16, actx) for i in range(2)]
            u_t = sb("u_t", [128, NDT * NT], F32, actx)
            w_t = sb("w_t", [128, NDT * NT], F32, actx)
            r_bf = [sb(f"r_bf{i}", [128, NDT * NT], BF16, actx) for i in range(2)]
            s_t = [sb(f"s_t{i}", [128, NDT * NT], F32, actx) for i in range(2)]
            out_f = sb("out_f", [128, NDT * NT], F32, actx)
            sq_t = sb("sq_t", [128, NDT * NT], BF16, actx)
            nrm_bf = sb("nrm_bf", [128, NDT * NT], BF16, actx)
            h_t = [sb(f"h_t{i}", [128, NDT * NT], F32, actx) for i in range(2)]
            hbfs = [sb(f"hbfs{i}", [128, NDT * NT], BF16, actx) for i in range(2)]
            kb_sb = [sb(f"kb_sb{i}", [128, NT], F32, actx) for i in range(2)]
            ksum_sb = sb("ksum_sb", [16, NT], BF16, actx)
            srow = sb("srow", [1, NT * 4], F32, actx)
            lgrow = [sb(f"lgrow{i}", [8, NT], F32, actx) for i in range(2)]
            ltok = sb("ltok", [128, 8], F32, actx)
            gtok = sb("gtok", [128, 8], F32, actx)
            gfm_t = sb("gfm_t", [8, NT], F32, actx)
            gcol = sb("gcol", [128, 8], F32, actx)
            ngsum = sb("ngsum", [8, 1], F32, actx)
            ngs8 = sb("ngs8", [128, 8], F32, actx)
            lt10 = sb("lt10", [128, 16], F32, actx)
            gtok_r = [sb(f"gtokr{i}", [128, 8 * NG], F32, actx) for i in range(2)]
            rrms_sb = sb("rrms_sb", [128, NT], F32, actx)

            pbank = [ps(f"pb{i}", actx) for i in range(3)]
            ptr = ps("ptr", actx)
            pks = actx.enter_context(nc.psum_tensor("pks", [128, 1024], BF16))
            pks2 = actx.enter_context(nc.psum_tensor("pks2", [128, 1024], BF16))
            pst = ps("pst", actx)
            pbc = ps("pbc", actx)

            bank_seq = [0]

            def next_bank():
                b = pbank[bank_seq[0] % 3]
                key = f"pbank{bank_seq[0] % 3}"
                bank_seq[0] += 1
                return b, key

            for nm, dst, srcw in (("wv", wv_bf, WvT), ("ww", ww_bf, WwT),
                                  ("wr", wr_bf, WrT), ("wo", wo_bf, WoT)):
                P.dma("gpsimd", lambda e, dst=dst, srcw=srcw: e.dma_start(
                    out=dst[:].rearrange("p (kt o) -> p kt o", kt=8),
                    in_=srcw.rearrange("(kt kp) o -> kp kt o", kp=128)),
                    writes=[nm])
            P.dma("sync", lambda e: e.dma_start(out=kbE_s[:], in_=kbE), writes=["kbE"])

            for half in range(2):
                P.dma("sync", lambda e, half=half: e.dma_start(
                    out=xs[0][:].rearrange("p (kt o) -> p kt o", kt=2),
                    in_=WkT.rearrange("(h kt kp) o -> h kp kt o", h=2, kp=128)[half][:, 0:2]),
                    writes=[("xs", 0)])
                P.dma("sync", lambda e, half=half: e.dma_start(
                    out=xs[1][:].rearrange("p (kt o) -> p kt o", kt=2),
                    in_=WkT.rearrange("(h kt kp) o -> h kp kt o", h=2, kp=128)[half][:, 2:4]),
                    writes=[("xs", 1)])
                for kt2 in range(4):
                    t = xs[kt2 // 2]
                    off = (kt2 % 2) * D
                    kt = half * 4 + kt2
                    for h in range(16):
                        def _rs(e, t=t, off=off, kt=kt, h=h):
                            with nc.allow_low_precision(reason="bf16 matmul operand (head-summed Wk)"):
                                return e.reduce_sum(
                                    wkts[:, kt * 16 + h: kt * 16 + h + 1],
                                    t[:, off + h * 64: off + (h + 1) * 64], axis=AX.X)
                        P.op("vector", _rs, reads=[("xs", kt2 // 2)], writes=["wkts"])

            for kt in range(8):
                P.op("tensor", lambda e, kt=kt: e.matmul(
                    pst[0:1, 0:8], ones_col[:], gwts[:, kt * 8:(kt + 1) * 8],
                    start=(kt == 0), stop=(kt == 7)),
                    reads=["gwts", "ones_col"], writes=["pst"])
            P.op("scalar", lambda e: e.copy(srow[0:1, 0:8], pst[0:1, 0:8]),
                 reads=["pst"], writes=["srow_gs"])
            P.op("tensor", lambda e: e.transpose(
                pbc[0:8, 0:1], srow[0:1, 0:8], idn[0:1, 0:1]),
                reads=["srow_gs", "idn"], writes=["pbc"])
            P.op("scalar", lambda e: e.mul(ngsum[:], pbc[0:8, 0:1], -1.0),
                 reads=["pbc"], writes=["ngsum"])
            P.op("scalar", lambda e: e.mul(srow[0:1, 8:16], srow[0:1, 0:8], -1.0),
                 reads=["srow_gs"], writes=["srow_ngs"])
            P.op("tensor", lambda e: e.matmul(
                pbc[:, 0:8], ones_row[:], srow[0:1, 8:16], start=True, stop=True),
                reads=["srow_ngs", "ones_row"], writes=["pbc"])
            P.op("vector", lambda e: e.tensor_copy(ngs8[:], pbc[:, 0:8]),
                 reads=["pbc"], writes=["ngs8"])

            def attn_frontA(i, nt_idx, is_pref, xsrc):
                sl = i % 2
                P.dma("gpsimd", lambda e: e.dma_start(
                    out=xs_bf[sl][:].rearrange("p (pt d) -> p pt d", pt=NPT),
                    in_=xsrc.rearrange("(n pt p) d -> n p pt d", p=128, pt=NPT)[nt_idx]),
                    writes=[("xsb", sl)])
                if not is_pref:
                    P.dma("sync", lambda e: e.dma_start(
                        out=xs[i % 3][:].rearrange("p (pt d) -> p pt d", pt=NPT),
                        in_=xsrc.rearrange("(n pt p) d -> n p pt d", p=128, pt=NPT)[nt_idx]),
                        writes=[("xs", i % 3)])
                for pt in range(NPT):
                    tb, tbkey = (pks, "pks") if pt % 2 == 0 else (pks2, "pks2")
                    for dt in range(NDT):
                        P.op("tensor", lambda e, pt=pt, dt=dt, tb=tb, sl=sl: e.transpose(
                            tb[:, dt * 128:(dt + 1) * 128],
                            xs_bf[sl][:, pt * D + dt * 128: pt * D + (dt + 1) * 128],
                            idnb[:]),
                            reads=[("xsb", sl), "idnb"], writes=[tbkey])
                    P.op("vector", lambda e, pt=pt, tb=tb, sl=sl: e.tensor_copy(
                        xTb[sl].rearrange("p (dt t) -> p dt t", dt=NDT)
                        [:, :, pt * 128:(pt + 1) * 128],
                        tb[:].rearrange("p (dt t) -> p dt t", dt=NDT)),
                        reads=[tbkey], writes=[("xTb", sl)])

            def attn_frontB(i, nt_idx, is_pref, tot):
                sl = i % 2
                for kt in range(NDT):
                    P.op("tensor", lambda e, kt=kt: e.matmul(
                        ptr[0:16, 0:NT], wkts[:, kt * 16:(kt + 1) * 16],
                        xTb[sl][:, kt * NT:(kt + 1) * NT],
                        start=(kt == 0), stop=(kt == NDT - 1)),
                        reads=["wkts", ("xTb", sl)],
                        writes=["ptr"])
                P.op("scalar", lambda e: e.copy(ksum_sb[:], ptr[0:16, 0:NT]),
                     reads=["ptr"], writes=["ksum_sb"])

                for dt in range(NDT):
                    bk, bkey = next_bank()
                    for kt in range(NDT):
                        P.op("tensor", lambda e, dt=dt, kt=kt, bk=bk: e.matmul(
                            bk[:, 0:NT], wv_bf[:, kt * D + dt * 128: kt * D + (dt + 1) * 128],
                            xTb[sl][:, kt * NT:(kt + 1) * NT],
                            start=(kt == 0), stop=(kt == NDT - 1)),
                            reads=["wv", ("xTb", sl)], writes=[bkey])
                    P.op("tensor", lambda e, dt=dt: e.matmul(
                        ptr[:, 0:NT], kbE_s[:, dt * 128:(dt + 1) * 128],
                        ksum_sb[:], start=True, stop=True),
                        reads=["kbE", "ksum_sb"], writes=["ptr"])
                    P.op("scalar", lambda e, dt=dt: e.copy(
                        kb_sb[dt % 2][:], ptr[:, 0:NT]),
                        reads=["ptr"], writes=[("kb", dt % 2)])
                    P.op("vector", lambda e, dt=dt, bk=bk: e.tensor_mul(
                        u_t[:, dt * NT:(dt + 1) * NT], bk[:, 0:NT], kb_sb[dt % 2][:]),
                        reads=[bkey, ("kb", dt % 2)], writes=[("u", dt)])
                for dt in range(NDT):
                    bk, bkey = next_bank()
                    for kt in range(NDT):
                        P.op("tensor", lambda e, dt=dt, kt=kt, bk=bk: e.matmul(
                            bk[:, 0:NT], ww_bf[:, kt * D + dt * 128: kt * D + (dt + 1) * 128],
                            xTb[sl][:, kt * NT:(kt + 1) * NT],
                            start=(kt == 0), stop=(kt == NDT - 1)),
                            reads=["ww", ("xTb", sl)], writes=[bkey])
                    P.op("scalar", lambda e, dt=dt, bk=bk: e.activation(
                        w_t[:, dt * NT:(dt + 1) * NT], bk[:, 0:NT], AF.Sigmoid),
                        reads=[bkey], writes=[("wsg", dt)])
                    P.op("vector", lambda e, dt=dt: e.tensor_scalar_mul(
                        w_t[:, dt * NT:(dt + 1) * NT],
                        w_t[:, dt * NT:(dt + 1) * NT], 0.99),
                        reads=[("wsg", dt)], writes=[("w", dt)])
                if not is_pref:
                    for dt in range(NDT):
                        bk, bkey = next_bank()
                        for kt in range(NDT):
                            P.op("tensor", lambda e, dt=dt, kt=kt, bk=bk: e.matmul(
                                bk[:, 0:NT], wr_bf[:, kt * D + dt * 128: kt * D + (dt + 1) * 128],
                                xTb[sl][:, kt * NT:(kt + 1) * NT],
                                start=(kt == 0), stop=(kt == NDT - 1)),
                                reads=["wr", ("xTb", sl)], writes=[bkey])
                        P.op("scalar", lambda e, dt=dt, bk=bk: e.activation(
                            r_bf[sl][:, dt * NT:(dt + 1) * NT], bk[:, 0:NT], AF.Sigmoid),
                            reads=[bkey], writes=[("r", sl, dt)])

                first_main = (not is_pref) and nt_idx == 0
                for dt in range(NDT):
                    if tot == 0:
                        init = 0.0
                        init_reads = []
                    elif first_main:
                        init = minit[:, dt:dt + 1]
                        init_reads = [("minit",)]
                    else:
                        psl = (i - 1) % 2
                        init = s_t[psl][:, dt * NT + NT - 1: dt * NT + NT]
                        init_reads = [("s", psl, dt)]
                    P.op("vector", lambda e, dt=dt, init=init: e.tensor_tensor_scan(
                        s_t[sl][:, dt * NT:(dt + 1) * NT],
                        w_t[:, dt * NT:(dt + 1) * NT],
                        u_t[:, dt * NT:(dt + 1) * NT],
                        init, ALU.mult, ALU.add),
                        reads=[("w", dt), ("u", dt)] + init_reads,
                        writes=[("s", sl, dt)])

                if is_pref and nt_idx == nt_pref - 1:
                    P.op("vector", lambda e: e.tensor_scalar(
                        minit[:],
                        s_t[sl].rearrange("p (dt t) -> p dt t", t=NT)[:, :, NT - 1:NT]
                        .rearrange("p dt one -> p (dt one)"),
                        mask_s[:], None, ALU.mult),
                        reads=[("s", sl, dt2) for dt2 in range(NDT)] + ["mask"],
                        writes=["minit"])

            def attn_back(i, nt_idx):
                sl = i % 2
                t0 = nt_idx * NT
                for pt in range(NPT):
                    tb, tbkey = (pst, "pst") if pt % 2 == 0 else (pbc, "pbc")
                    for dt4 in range(NDT // 4):
                        for j in range(4):
                            dt = dt4 * 4 + j
                            P.op("tensor", lambda e, pt=pt, dt=dt, j=j, tb=tb, i=i: e.transpose(
                                tb[:, j * 128:(j + 1) * 128],
                                xs[i % 3][:, pt * D + dt * 128: pt * D + (dt + 1) * 128],
                                idn[:]),
                                reads=[("xs", i % 3), "idn"], writes=[tbkey])
                        P.op("vector", lambda e, pt=pt, dt4=dt4, tb=tb: e.tensor_copy(
                            xT.rearrange("p (dt t) -> p dt t", dt=NDT)
                            [:, dt4 * 4:(dt4 + 1) * 4, pt * 128:(pt + 1) * 128],
                            tb[:, 0:512].rearrange("p (j t) -> p j t", j=4)),
                            reads=[tbkey], writes=[("xT", dt4)])
                xT_reads = [("xT", dt4) for dt4 in range(NDT // 4)]
                for dt in range(NDT):
                    P.op("vector", lambda e, dt=dt: e.tensor_mul(
                        out_f[:, dt * NT:(dt + 1) * NT],
                        s_t[sl][:, dt * NT:(dt + 1) * NT],
                        r_bf[sl][:, dt * NT:(dt + 1) * NT]),
                        reads=[("s", sl, dt), ("r", sl, dt)], writes=[("outf", dt)])
                    P.op("scalar", lambda e, dt=dt: e.activation(
                        sq_t[:, dt * NT:(dt + 1) * NT],
                        out_f[:, dt * NT:(dt + 1) * NT], AF.Square),
                        reads=[("outf", dt)], writes=[("sq", dt)])
                for dt in range(NDT):
                    P.op("tensor", lambda e, dt=dt: e.matmul(
                        pst[0:1, 0:NT], ones_colb[:], sq_t[:, dt * NT:(dt + 1) * NT],
                        start=(dt == 0), stop=(dt == NDT - 1)),
                        reads=[("sq", dt), "ones_colb"], writes=["pst"])
                P.op("scalar", lambda e: e.activation(
                    srow[0:1, 0:NT], pst[0:1, 0:NT], AF.Sqrt,
                    bias=consts[0:1, 0:1], scale=1.0 / D),
                    reads=["pst", "consts"], writes=["srow_rms0"])
                P.op("vector", lambda e: e.reciprocal(
                    srow[0:1, NT:2 * NT], srow[0:1, 0:NT]),
                    reads=["srow_rms0"], writes=["srow_rms1"])
                P.op("scalar", lambda e: e.copy(nrm_bf[:], out_f[:]),
                     reads=[("outf", dt2) for dt2 in range(NDT)], writes=["nrm"])
                wo_banks = []
                for dt in range(NDT):
                    bk, bkey = next_bank()
                    wo_banks.append((bk, bkey))
                    for kt in range(NDT):
                        P.op("tensor", lambda e, dt=dt, kt=kt, bk=bk: e.matmul(
                            bk[:, 0:NT], wo_bf[:, kt * D + dt * 128: kt * D + (dt + 1) * 128],
                            nrm_bf[:, kt * NT:(kt + 1) * NT],
                            start=(kt == 0), stop=(kt == NDT - 1)),
                            reads=["wo", "nrm"],
                            writes=[bkey])
                    if dt == 1:
                        P.op("tensor", lambda e: e.matmul(
                            pbc[:, 0:NT], ones_row[:], srow[0:1, NT:2 * NT],
                            start=True, stop=True),
                            reads=["srow_rms1", "ones_row"], writes=["pbc"])
                        P.op("vector", lambda e: e.tensor_copy(
                            rrms_sb[:], pbc[:, 0:NT]),
                            reads=["pbc"], writes=["rrms"])
                    if dt >= 1:
                        pd = dt - 1
                        pbk, pbkey = wo_banks[pd]
                        P.op("vector", lambda e, pd=pd, pbk=pbk: e.tensor_mul(
                            out_f[:, pd * NT:(pd + 1) * NT], pbk[:, 0:NT], rrms_sb[:]),
                            reads=[pbkey, "rrms"], writes=[("outf", pd)])
                        P.op("vector", lambda e, pd=pd: e.tensor_add(
                            out_f[:, pd * NT:(pd + 1) * NT],
                            out_f[:, pd * NT:(pd + 1) * NT],
                            xT[:, pd * NT:(pd + 1) * NT]),
                            reads=[("outf", pd)] + xT_reads, writes=[("outf", pd)])
                        P.op("scalar", lambda e, pd=pd: e.activation(
                            sq_t[:, pd * NT:(pd + 1) * NT],
                            out_f[:, pd * NT:(pd + 1) * NT], AF.Square),
                            reads=[("outf", pd)], writes=[("sq", pd)])
                pd = NDT - 1
                pbk, pbkey = wo_banks[pd]
                P.op("vector", lambda e, pd=pd, pbk=pbk: e.tensor_mul(
                    out_f[:, pd * NT:(pd + 1) * NT], pbk[:, 0:NT], rrms_sb[:]),
                    reads=[pbkey, "rrms"], writes=[("outf", pd)])
                P.op("vector", lambda e, pd=pd: e.tensor_add(
                    out_f[:, pd * NT:(pd + 1) * NT],
                    out_f[:, pd * NT:(pd + 1) * NT],
                    xT[:, pd * NT:(pd + 1) * NT]),
                    reads=[("outf", pd)] + xT_reads, writes=[("outf", pd)])
                P.op("scalar", lambda e, pd=pd: e.activation(
                    sq_t[:, pd * NT:(pd + 1) * NT],
                    out_f[:, pd * NT:(pd + 1) * NT], AF.Square),
                    reads=[("outf", pd)], writes=[("sq", pd)])
                for dt in range(NDT):
                    P.op("tensor", lambda e, dt=dt: e.matmul(
                        pst[0:1, 0:NT], ones_col[:], out_f[:, dt * NT:(dt + 1) * NT],
                        start=(dt == 0), stop=(dt == NDT - 1)),
                        reads=[("outf", dt), "ones_col"], writes=["pst"])
                for dt in range(NDT):
                    P.op("tensor", lambda e, dt=dt: e.matmul(
                        pst[0:1, NT:2 * NT], ones_colb[:], sq_t[:, dt * NT:(dt + 1) * NT],
                        start=(dt == 0), stop=(dt == NDT - 1)),
                        reads=[("sq", dt), "ones_colb"], writes=["pst"])
                for kt in range(NDT):
                    P.op("tensor", lambda e, kt=kt: e.matmul(
                        ptr[0:8, 0:NT], gwts[:, kt * 8:(kt + 1) * 8],
                        out_f[:, kt * NT:(kt + 1) * NT],
                        start=(kt == 0), stop=(kt == NDT - 1)),
                        reads=[("outf", kt), "gwts"], writes=["ptr"])
                P.op("scalar", lambda e: e.mul(srow[0:1, 0:NT], pst[0:1, 0:NT], 1.0 / D),
                     reads=["pst"], writes=["srow_m"])
                P.op("scalar", lambda e: e.activation(
                    srow[0:1, NT:2 * NT], srow[0:1, 0:NT], AF.Square),
                    reads=["srow_m"], writes=["srow_m2"])
                P.op("scalar", lambda e: e.mul(srow[0:1, 2 * NT:3 * NT], pst[0:1, NT:2 * NT], 1.0 / D),
                     reads=["pst"], writes=["srow_q"])
                P.op("scalar", lambda e, sl=sl: e.copy(lgrow[sl][:], ptr[0:8, 0:NT]),
                     reads=["ptr"], writes=[("lgrow", sl)])
                P.op("vector", lambda e: e.tensor_sub(
                    srow[0:1, 2 * NT:3 * NT], srow[0:1, 2 * NT:3 * NT], srow[0:1, NT:2 * NT]),
                    reads=["srow_q", "srow_m2"], writes=["srow_var"])
                P.op("scalar", lambda e: e.activation(
                    srow[0:1, 3 * NT:4 * NT], srow[0:1, 2 * NT:3 * NT], AF.Sqrt,
                    bias=consts[0:1, 1:2]),
                    reads=["srow_var", "consts"], writes=["srow_std"])
                P.op("vector", lambda e: e.reciprocal(
                    srow[0:1, 2 * NT:3 * NT], srow[0:1, 3 * NT:4 * NT]),
                    reads=["srow_std"], writes=["srow_rstd"])
                P.op("tensor", lambda e: e.matmul(
                    pbc[:, 0:NT], ones_row[:], srow[0:1, 0:NT], start=True, stop=True),
                    reads=["srow_m", "ones_row"], writes=["pbc"])
                P.op("tensor", lambda e: e.matmul(
                    pst[:, 0:NT], ones_row[:], srow[0:1, 2 * NT:3 * NT], start=True, stop=True),
                    reads=["srow_rstd", "ones_row", "pst"], writes=["pst"])
                for dt in range(NDT):
                    P.op("vector", lambda e, dt=dt: e.tensor_sub(
                        h_t[sl][:, dt * NT:(dt + 1) * NT],
                        out_f[:, dt * NT:(dt + 1) * NT], pbc[:, 0:NT]),
                        reads=[("outf", dt), "pbc"], writes=[("h", sl, dt)])
                for dt in range(NDT):
                    P.op("vector", lambda e, dt=dt: e.tensor_mul(
                        h_t[sl][:, dt * NT:(dt + 1) * NT],
                        h_t[sl][:, dt * NT:(dt + 1) * NT], pst[:, 0:NT]),
                        reads=[("h", sl, dt), "pst"], writes=[("h", sl, dt)])
                h_reads = [("h", sl, dt2) for dt2 in range(NDT)]
                P.dma("sync", lambda e, t0=t0: e.dma_start(
                    out=hscr.rearrange("(dt kp) T -> kp dt T", kp=128)[:, :, t0:t0 + NT],
                    in_=h_t[sl][:].rearrange("p (dt t) -> p dt t", dt=NDT)),
                    reads=h_reads, writes=[("hscr", nt_idx)])
                P.op("scalar", lambda e: e.copy(hbfs[sl][:], h_t[sl][:]),
                     reads=h_reads, writes=[("hbfs", sl)])
                P.dma("sync", lambda e, t0=t0: e.dma_start(
                    out=hscr_bf.rearrange("(dt kp) T -> kp dt T", kp=128)[:, :, t0:t0 + NT],
                    in_=hbfs[sl][:].rearrange("p (dt t) -> p dt t", dt=NDT)),
                    reads=[("hbfs", sl)], writes=[("hscr_bf", nt_idx)])

                for g in range(NG):
                    P.op("tensor", lambda e, g=g, sl=sl: e.transpose(
                        pbc[:, 0:8], lgrow[sl][:, g * 128:(g + 1) * 128], idn[0:8, 0:8]),
                        reads=[("lgrow", sl), "idn"], writes=["pbc"])
                    P.op("tensor", lambda e, g=g: e.transpose(
                        pbc[:, 8:9], srow[0:1, g * 128:(g + 1) * 128], idn[0:1, 0:1]),
                        reads=["srow_m", "idn"], writes=["pbc"])
                    P.op("tensor", lambda e, g=g: e.transpose(
                        pbc[:, 9:10], srow[0:1, 2 * NT + g * 128: 2 * NT + (g + 1) * 128],
                        idn[0:1, 0:1]),
                        reads=["srow_rstd", "idn"], writes=["pbc"])
                    P.op("vector", lambda e: e.tensor_copy(lt10[:, 0:10], pbc[:, 0:10]),
                         reads=["pbc"], writes=["lt10"])
                    P.op("vector", lambda e: e.scalar_tensor_tensor(
                        ltok[:], ngs8[:], lt10[:, 8:9], lt10[:, 0:8], ALU.mult, ALU.add),
                        reads=["lt10", "ngs8"], writes=["ltok"])
                    P.op("vector", lambda e: e.tensor_scalar(
                        ltok[:], ltok[:], lt10[:, 9:10], None, ALU.mult),
                        reads=["ltok", "lt10"], writes=["ltok"])
                    P.op("vector", lambda e: e.reduce_max(gcol[:, 0:1], ltok[:], axis=AX.X),
                         reads=["ltok"], writes=["g_m1"])
                    P.op("vector", lambda e: e.tensor_scalar_mul(gcol[:, 1:2], gcol[:, 0:1], -1.0),
                         reads=["g_m1"], writes=["g_nm1"])
                    P.op("vector", lambda e: e.tensor_scalar(
                        gtok[:], ltok[:], gcol[:, 0:1], None, ALU.is_ge),
                        reads=["ltok", "g_m1"], writes=["gtok"])
                    P.op("vector", lambda e: e.scalar_tensor_tensor(
                        gtok[:], gtok[:], -1e30, ltok[:], ALU.mult, ALU.add),
                        reads=["gtok", "ltok"], writes=["gtok"])
                    P.op("vector", lambda e: e.reduce_max(gcol[:, 2:3], gtok[:], axis=AX.X),
                         reads=["gtok"], writes=["g_m2"])
                    P.op("scalar", lambda e: e.activation(
                        gtok[:], ltok[:], AF.Exp, bias=gcol[:, 1:2], accum_out=gcol[:, 3:4]),
                        reads=["ltok", "g_nm1"], writes=["gtok", "g_z"])
                    P.op("vector", lambda e: e.reciprocal(gcol[:, 4:5], gcol[:, 3:4]),
                         reads=["g_z"], writes=["g_rz"])
                    P.op("vector", lambda e: e.tensor_scalar(
                        ltok[:], ltok[:], gcol[:, 2:3], None, ALU.is_ge),
                        reads=["ltok", "g_m2"], writes=["ltok"])
                    P.op("vector", lambda e: e.tensor_mul(gtok[:], gtok[:], ltok[:]),
                         reads=["gtok", "ltok"], writes=["gtok"])
                    P.op("vector", lambda e, g=g, sl=sl: e.tensor_scalar(
                        gtok_r[sl][:, g * 8:(g + 1) * 8], gtok[:], gcol[:, 4:5], 0.5,
                        ALU.mult, ALU.mult),
                        reads=["gtok", "g_rz"], writes=[("gtokr", sl, g)])

            def attn_gates(i, nt_idx):
                sl = i % 2
                t0 = nt_idx * NT
                for g in range(NG):
                    P.op("tensor", lambda e, g=g, sl=sl: e.transpose(
                        pbc[0:8, 0:128], gtok_r[sl][:, g * 8:(g + 1) * 8], idn[:]),
                        reads=[("gtokr", sl, g), "idn"], writes=["pbc"])
                    P.op("scalar", lambda e, g=g: e.copy(
                        gfm_t[:, g * 128:(g + 1) * 128], pbc[0:8, 0:128]),
                        reads=["pbc"], writes=[("gfm", g)])
                P.dma("sync", lambda e, t0=t0: e.dma_start(
                    out=gscr[:, t0:t0 + NT], in_=gfm_t[:]),
                    reads=[("gfm", g) for g in range(NG)], writes=[("gscr", nt_idx)])

            tiles = [(ntp, True) for ntp in range(nt_pref)] + \
                    [(ntm, False) for ntm in range(nt_main)]
            stages = []
            for i, (nt_idx, is_pref) in enumerate(tiles):
                stages.append((i, nt_idx, is_pref))
            n_tiles = len(stages)
            for i in range(n_tiles + 3):
                if i < n_tiles:
                    ii, nt_idx, is_pref = stages[i]
                    attn_frontA(ii, nt_idx, is_pref, x_pref if is_pref else x_main)
                if 1 <= i <= n_tiles:
                    ii, nt_idx, is_pref = stages[i - 1]
                    attn_frontB(ii, nt_idx, is_pref, i - 1)
                if 2 <= i <= n_tiles + 1:
                    ii, nt_idx, is_pref = stages[i - 2]
                    if not is_pref:
                        attn_back(ii, nt_idx)
                if i >= 3:
                    ii, nt_idx, is_pref = stages[i - 3]
                    if not is_pref:
                        attn_gates(ii, nt_idx)

        P.barrier()

        with contextlib.ExitStack() as fctx:
            h_bfb = [sb(f"h_bfb{i}", [128, NDT * NB], BF16, fctx) for i in range(2)]
            h_res = [sb(f"h_res{i}", [128, NSUB], F32, fctx) for i in range(2)]
            gu = sb("gu", [128, NFC * NB], BF16, fctx)
            y_acc = sb("y_acc", [128, NDT * NB], F32, fctx)
            ws1 = [sb(f"ws1_{i}", [128, 2048], BF16, fctx) for i in range(4)]
            ws2 = [sb(f"ws2_{i}", [128, 4096], BF16, fctx) for i in range(2)]
            gbc = [sb(f"gbc{i}", [128, NSUB], F32, fctx) for i in range(2 * max(n_ns, 1))]
            sg_sb = [sb(f"sg{i}", [128, NSUB], BF16, fctx) for i in range(2)]
            ug_sb = [sb(f"ug{i}", [128, NSUB], BF16, fctx) for i in range(2)]
            ge_row = sb("ge_row", [1, NB], F32, fctx)
            r2tok = [sb(f"r2tok{i}", [128, D], F32, fctx) for i in range(2)]
            otok = [sb(f"otok{i}", [128, D], F32, fctx) for i in range(2)]
            ccol = sb("ccol", [128, 8], F32, fctx)

            psg = [ps(f"psg{i}", fctx) for i in range(2)]
            psu = [ps(f"psu{i}", fctx) for i in range(2)]
            psy = [ps(f"psy{i}", fctx) for i in range(2)]
            psx = ps("psx", fctx)
            psx2 = ps("psx2", fctx)

            def load_hbfb(nb2):
                b0x = nb2 * NB
                hbx = nb2 % 2
                P.dma("sync", lambda e, b0x=b0x, hbx=hbx: e.dma_start(
                    out=h_bfb[hbx][:].rearrange("p (dt t) -> p dt t", dt=NDT),
                    in_=hscr_bf.rearrange("(dt kp) T -> kp dt T", kp=128)[:, :, b0x:b0x + NB]),
                    reads=[("hscr_bf", q) for q in
                           range(nb2 * (NB // NT), (nb2 + 1) * (NB // NT))],
                    writes=[("h_bfb", hbx)])

            load_hbfb(0)
            for nb in range(n_nb):
                b0 = nb * NB
                hb = nb % 2
                hq = list(range(nb * (NB // NT), (nb + 1) * (NB // NT)))

                def emit_gbc(ee2, b0=b0, hq=hq):
                    P.dma("sync", lambda e, ee2=ee2, b0=b0: e.dma_start(
                        out=ge_row[:], in_=gscr[ee2:ee2 + 1, b0:b0 + NB]),
                        reads=[("gscr", q) for q in hq],
                        writes=["ge_row"])
                    sl2 = ee2 % 2
                    for ns in range(n_ns):
                        P.op("tensor", lambda e, ns=ns: e.matmul(
                            psx2[:, 0:NSUB], ones_row[:],
                            ge_row[0:1, ns * NSUB:(ns + 1) * NSUB],
                            start=True, stop=True),
                            reads=["ge_row", "ones_row"], writes=["psx2"])
                        P.op("vector", lambda e, ns=ns, sl2=sl2: e.tensor_copy(
                            gbc[sl2 * n_ns + ns][:], psx2[:, 0:NSUB]),
                            reads=["psx2"], writes=[("gbc", sl2, ns)])

                if nb == 0:
                    emit_gbc(0)
                for ee in range(NEXP):
                    for fc in range(NFC):
                        ci = ee * NFC + fc
                        slw = ci % 4
                        if fc == 2 and ee + 1 < 8:
                            emit_gbc(ee + 1)
                        if fc == 4 and ee == 4 and nb + 1 < n_nb:
                            load_hbfb(nb + 1)
                        if fc == 2 and ee == 8 and nb + 1 < n_nb:
                            emit_gbc(0, b0=(nb + 1) * NB,
                                     hq=list(range((nb + 1) * (NB // NT),
                                                   (nb + 2) * (NB // NT))))
                        if nb == 0:
                            P.dma("gpsimd", lambda e, ci=ci, slw=slw: e.dma_start(
                                out=ws1[slw][:], in_=Wst1[ci]),
                                writes=[("ws1", slw)])
                            P.dma("sync", lambda e, ci=ci, slw=slw: e.dma_start(
                                out=wbs1[ci], in_=ws1[slw][:]),
                                reads=[("ws1", slw)], writes=[("wbs1", ci)])
                        else:
                            P.dma("sync", lambda e, ci=ci, slw=slw: e.dma_start(
                                out=ws1[slw][:], in_=wbs1[ci]),
                                reads=[("wbs1", ci)], writes=[("ws1", slw)])
                        for ns in range(n_ns):
                            pgi = psg[ns % 2]
                            pui = psu[ns % 2]
                            for kt in range(NDT):
                                P.op("tensor", lambda e, ns=ns, kt=kt, slw=slw, pgi=pgi, hb=hb: e.matmul(
                                    pgi[:, 0:NSUB], ws1[slw][:, kt * 128:(kt + 1) * 128],
                                    h_bfb[hb][:, kt * NB + ns * NSUB: kt * NB + (ns + 1) * NSUB],
                                    start=(kt == 0), stop=(kt == NDT - 1)),
                                    reads=[("ws1", slw), ("h_bfb", hb)], writes=[f"psg{ns % 2}"])
                            for kt in range(NDT):
                                P.op("tensor", lambda e, ns=ns, kt=kt, slw=slw, pui=pui, hb=hb: e.matmul(
                                    pui[:, 0:NSUB], ws1[slw][:, 1024 + kt * 128: 1024 + (kt + 1) * 128],
                                    h_bfb[hb][:, kt * NB + ns * NSUB: kt * NB + (ns + 1) * NSUB],
                                    start=(kt == 0), stop=(kt == NDT - 1)),
                                    reads=[("ws1", slw), ("h_bfb", hb)], writes=[f"psu{ns % 2}"])
                            P.op("scalar", lambda e, ns=ns, pgi=pgi: e.activation(
                                sg_sb[ns % 2][:], pgi[:, 0:NSUB], AF.Silu),
                                reads=[f"psg{ns % 2}"], writes=[("sg", ns % 2)])
                            if ee < 8:
                                P.op("vector", lambda e, ns=ns, pui=pui, ee=ee: e.tensor_mul(
                                    ug_sb[ns % 2][:], pui[:, 0:NSUB],
                                    gbc[(ee % 2) * n_ns + ns][:]),
                                    reads=[f"psu{ns % 2}", ("gbc", ee % 2, ns)],
                                    writes=[("ug", ns % 2)])
                            else:
                                P.op("vector", lambda e, ns=ns, pui=pui: e.tensor_scalar(
                                    ug_sb[ns % 2][:], pui[:, 0:NSUB], 0.0625, None, ALU.mult),
                                    reads=[f"psu{ns % 2}"], writes=[("ug", ns % 2)])
                            P.op("vector", lambda e, fc=fc, ns=ns: e.tensor_mul(
                                gu[:, fc * NB + ns * NSUB: fc * NB + (ns + 1) * NSUB],
                                sg_sb[ns % 2][:], ug_sb[ns % 2][:]),
                                reads=[("sg", ns % 2), ("ug", ns % 2)],
                                writes=[("gu", fc, ns)])
                    for dt in range(NDT):
                        di = ee * NDT + dt
                        slw = di % 2
                        if nb == 0:
                            P.dma("gpsimd", lambda e, di=di, slw=slw: e.dma_start(
                                out=ws2[slw][:], in_=Wst2[di]),
                                writes=[("ws2", slw)])
                            P.dma("sync", lambda e, di=di, slw=slw: e.dma_start(
                                out=wbs2[di], in_=ws2[slw][:]),
                                reads=[("ws2", slw)], writes=[("wbs2", di)])
                        else:
                            P.dma("sync", lambda e, di=di, slw=slw: e.dma_start(
                                out=ws2[slw][:], in_=wbs2[di]),
                                reads=[("wbs2", di)], writes=[("ws2", slw)])
                        for ns in range(n_ns):
                            pyi = psy[ns % 2]
                            gu_reads = [("gu", fkt, ns) for fkt in range(NFC)]
                            for fkt in range(NFC):
                                P.op("tensor", lambda e, ns=ns, fkt=fkt, slw=slw, pyi=pyi: e.matmul(
                                    pyi[:, 0:NSUB], ws2[slw][:, fkt * 128:(fkt + 1) * 128],
                                    gu[:, fkt * NB + ns * NSUB: fkt * NB + (ns + 1) * NSUB],
                                    start=(fkt == 0), stop=(fkt == NFC - 1)),
                                    reads=[("ws2", slw)] + (gu_reads if fkt == 0 else [("gu", fkt, ns)]),
                                    writes=[f"psy{ns % 2}"])
                            if ee == 0:
                                P.op("vector", lambda e, dt=dt, ns=ns, pyi=pyi: e.tensor_copy(
                                    y_acc[:, dt * NB + ns * NSUB: dt * NB + (ns + 1) * NSUB],
                                    pyi[:, 0:NSUB]),
                                    reads=[f"psy{ns % 2}"], writes=[("y", dt, ns)])
                            else:
                                P.op("vector", lambda e, dt=dt, ns=ns, pyi=pyi: e.tensor_add(
                                    y_acc[:, dt * NB + ns * NSUB: dt * NB + (ns + 1) * NSUB],
                                    y_acc[:, dt * NB + ns * NSUB: dt * NB + (ns + 1) * NSUB],
                                    pyi[:, 0:NSUB]),
                                    reads=[f"psy{ns % 2}", ("y", dt, ns)],
                                    writes=[("y", dt, ns)])
                            if ee == NEXP - 1:
                                hs = (dt * n_ns + ns) % 2
                                P.dma("sync", lambda e, dt=dt, ns=ns, hs=hs, b0=b0: e.dma_start(
                                    out=h_res[hs][:],
                                    in_=hscr[dt * 128:(dt + 1) * 128,
                                             b0 + ns * NSUB: b0 + (ns + 1) * NSUB]),
                                    reads=[("hscr", q) for q in hq],
                                    writes=[("h_res", hs)])
                                P.op("vector", lambda e, dt=dt, ns=ns, hs=hs: e.tensor_add(
                                    y_acc[:, dt * NB + ns * NSUB: dt * NB + (ns + 1) * NSUB],
                                    y_acc[:, dt * NB + ns * NSUB: dt * NB + (ns + 1) * NSUB],
                                    h_res[hs][:]),
                                    reads=[("y", dt, ns), ("h_res", hs)], writes=[("y", dt, ns)])

                for tg in range(NB // 128):
                    stok = tg % 2
                    yreads = [("y", dt, (tg * 128) // NSUB) for dt in range(NDT)]
                    for half in range(2):
                        for j in range(4):
                            dt = half * 4 + j
                            P.op("tensor", lambda e, dt=dt, j=j, tg=tg: e.transpose(
                                psx[:, j * 128:(j + 1) * 128],
                                y_acc[:, dt * NB + tg * 128: dt * NB + (tg + 1) * 128],
                                idn[:]),
                                reads=yreads if (half == 0 and j == 0) else
                                [("y", dt, (tg * 128) // NSUB)],
                                writes=["psx"])
                        P.op("vector", lambda e, half=half, stok=stok: e.tensor_copy(
                            r2tok[stok][:, half * 512:(half + 1) * 512], psx[:, 0:512]),
                            reads=["psx"], writes=[("r2", stok, half)])
                    r2r = [("r2", stok, 0), ("r2", stok, 1)]
                    P.op("vector", lambda e, stok=stok: e.reduce_sum(
                        ccol[:, 0:1], r2tok[stok][:], axis=AX.X),
                        reads=r2r, writes=["cc_s"])
                    P.op("scalar", lambda e, stok=stok: e.activation(
                        otok[stok][:], r2tok[stok][:], AF.Square, accum_out=ccol[:, 1:2]),
                        reads=r2r, writes=[("otok", stok), "cc_q"])
                    P.op("scalar", lambda e: e.mul(ccol[:, 2:3], ccol[:, 0:1], 1.0 / D),
                         reads=["cc_s"], writes=["cc_m"])
                    P.op("scalar", lambda e: e.activation(
                        ccol[:, 3:4], ccol[:, 2:3], AF.Square),
                        reads=["cc_m"], writes=["cc_m2"])
                    P.op("scalar", lambda e: e.mul(ccol[:, 4:5], ccol[:, 1:2], 1.0 / D),
                         reads=["cc_q"], writes=["cc_q2"])
                    P.op("vector", lambda e: e.tensor_sub(
                        ccol[:, 5:6], ccol[:, 4:5], ccol[:, 3:4]),
                        reads=["cc_q2", "cc_m2"], writes=["cc_var"])
                    P.op("scalar", lambda e: e.activation(
                        ccol[:, 6:7], ccol[:, 5:6], AF.Sqrt, bias=consts[:, 1:2]),
                        reads=["cc_var", "consts"], writes=["cc_std"])
                    P.op("vector", lambda e: e.reciprocal(ccol[:, 7:8], ccol[:, 6:7]),
                         reads=["cc_std"], writes=["cc_rstd"])
                    P.op("vector", lambda e, stok=stok: e.tensor_scalar(
                        otok[stok][:], r2tok[stok][:], ccol[:, 2:3], ccol[:, 7:8],
                        ALU.subtract, ALU.mult),
                        reads=r2r + ["cc_m", "cc_rstd"], writes=[("otok", stok)])
                    st = P.dma("gpsimd", lambda e, b0=b0, tg=tg, stok=stok: e.dma_start(
                        out=out_t[b0 + tg * 128: b0 + (tg + 1) * 128, :],
                        in_=otok[stok][:]),
                        reads=[("otok", stok)], writes=[("out", nb, tg)])
                    out_stores.append(st)

        P.finalize_values()
        with nc.Block() as block:
            @block.sync
            def _(e):
                P.emit_engine("sync", e)
                P.final_wait("sync", e, out_stores)

            @block.gpsimd
            def _(e):
                P.emit_engine("gpsimd", e)

            @block.tensor
            def _(e):
                P.emit_engine("tensor", e)

            @block.scalar
            def _(e):
                P.emit_engine("scalar", e)

            @block.vector
            def _(e):
                P.emit_engine("vector", e)

    return nc, "out"


def prep_inputs(inputs, T_main=4096, T_pref=4096):
    x = np.ascontiguousarray(inputs["L0_attn_x"])
    B, T, D = x.shape
    NFC, NDT, NEXP = 32, 8, 9

    def t(a):
        return np.ascontiguousarray(np.asarray(a).T)

    WrT, WkT, WvT, WwT, WoT = (t(inputs["L0_attn_Wr"]), t(inputs["L0_attn_Wk"]),
                               t(inputs["L0_attn_Wv"]), t(inputs["L0_attn_Ww"]),
                               t(inputs["L0_attn_Wo"]))
    gWT = t(inputs["L0_ffn_gate_W"])

    w1 = [np.asarray(inputs["L0_ffn_w1_e"][e]) for e in range(8)]
    w3 = [np.asarray(inputs["L0_ffn_w3_e"][e]) for e in range(8)]
    w2 = [np.asarray(inputs["L0_ffn_w2_e"][e]) for e in range(8)]
    w1.append(np.concatenate([inputs["L0_ffn_w1_s"][n] for n in range(8)], axis=1))
    w3.append(np.concatenate([inputs["L0_ffn_w3_s"][n] for n in range(8)], axis=1))
    w2.append(np.concatenate([inputs["L0_ffn_w2_s"][n] for n in range(8)], axis=0))

    st1 = np.empty((NEXP * NFC, 128, 2048), np.float32)
    st2 = np.empty((NEXP * NDT, 128, 4096), np.float32)
    for e in range(NEXP):
        a1 = w1[e].reshape(8, 128, NFC, 128).transpose(2, 1, 0, 3).reshape(NFC, 128, 1024)
        a3 = w3[e].reshape(8, 128, NFC, 128).transpose(2, 1, 0, 3).reshape(NFC, 128, 1024)
        st1[e * NFC:(e + 1) * NFC] = np.concatenate([a1, a3], axis=2)
        st2[e * NDT:(e + 1) * NDT] = (w2[e].reshape(NFC, 128, NDT, 128)
                                      .transpose(2, 1, 0, 3).reshape(NDT, 128, 4096))

    import ml_dtypes
    kbE = np.zeros((16, D), ml_dtypes.bfloat16)
    for c in range(D):
        kbE[(c % 128 >= 64) + 2 * (c // 128), c] = 1.0

    common = {
        "WrT": WrT, "WkT": WkT, "WvT": WvT, "WwT": WwT, "WoT": WoT,
        "gWT": np.ascontiguousarray(gWT), "Wst1": st1, "Wst2": st2, "kbE": kbE,
        "idn": np.eye(128, dtype=np.float32),
        "idnb": np.eye(128, dtype=ml_dtypes.bfloat16),
        "ones_col": np.ones((128, 1), np.float32),
        "ones_colb": np.ones((128, 1), ml_dtypes.bfloat16),
        "ones_row": np.ones((1, 128), np.float32),
    }
    n_half = T // T_main
    in_maps = []
    for core in range(B * n_half):
        b = core // n_half
        half = core % n_half
        m = dict(common)
        m["x_main"] = np.ascontiguousarray(x[b, half * T_main:(half + 1) * T_main])
        m["x_pref"] = np.ascontiguousarray(x[b, 0:max(T_pref, 1)])
        m["mask"] = np.full((128, 1), float(half > 0), np.float32)
        in_maps.append(m)
    return in_maps


def kernel(**inputs):
    T_main, T_pref = 4096, 4096
    x = inputs["L0_attn_x"]
    B, T, D = x.shape
    nc, out_name = build_megakernel(T_main=T_main, T_pref=T_pref)
    in_maps = prep_inputs(inputs, T_main=T_main, T_pref=T_pref)
    res = run_bass_kernel_spmd(nc, in_maps, core_ids=list(range(8)))
    out = np.empty((B, T, D), np.float32)
    n_half = T // T_main
    for core in range(8):
        b, half = core // n_half, core % n_half
        out[b, half * T_main:(half + 1) * T_main] = res.results[core][out_name]
    return out


# version 15
# speedup vs baseline: 1.1797x; 1.1797x over previous
import contextlib

import numpy as np
import concourse.bass as bass
import concourse.mybir as mybir
from concourse.bass_utils import run_bass_kernel_spmd

F32 = mybir.dt.float32
BF16 = mybir.dt.bfloat16
AF = mybir.ActivationFunctionType
ALU = mybir.AluOpType
AX = mybir.AxisListType

EPS_RMS = float(np.finfo(np.float32).eps)
EPS_LN = 1e-5

ENGINES = ("sync", "gpsimd", "tensor", "scalar", "vector")
DMA_QUEUES = ("sync", "gpsimd")
N_DMA_SLOTS = 8


class Prog:

    def __init__(self, nc, ctx):
        self.nc = nc
        self.ops = {e: [] for e in ENGINES}
        self.regions = {}
        self.psem = {e: ctx.enter_context(nc.semaphore(f"prog_{e}"))
                     for e in ("tensor", "scalar", "vector", "gpsimd")}
        self.dsem = {q: [ctx.enter_context(nc.semaphore(f"dma_{q}_{i}"))
                         for i in range(N_DMA_SLOTS)] for q in DMA_QUEUES}
        self.dma_count = {q: 0 for q in DMA_QUEUES}
        self.pending_barrier = {e: [] for e in ENGINES}

    def _region(self, key):
        r = self.regions.get(key)
        if r is None:
            r = {"w": None, "r": []}
            self.regions[key] = r
        return r

    def _record(self, engine, emit, reads, writes, is_dma, dma_slot=None,
                dma_target=None):
        op = {
            "engine": engine, "emit": emit, "idx": len(self.ops[engine]),
            "flag": False, "value": None, "deps": [],
            "is_dma": is_dma, "dma_slot": dma_slot, "dma_target": dma_target,
        }
        deps = list(self.pending_barrier[engine])
        self.pending_barrier[engine] = []
        for key in reads:
            r = self._region(key)
            if r["w"] is not None:
                deps.append(r["w"])
            r["r"].append(op)
        for key in writes:
            r = self._region(key)
            deps.extend(r["r"])
            if r["w"] is not None:
                deps.append(r["w"])
            r["w"] = op
            r["r"] = []
        seen = set()
        for d in deps:
            if d is op or id(d) in seen:
                continue
            seen.add(id(d))
            if d["engine"] == engine and not d["is_dma"] and not is_dma \
                    and engine == "tensor" and not d.get("bar"):
                continue
            if not d["is_dma"]:
                d["flag"] = True
            op["deps"].append(d)
        self.ops[engine].append(op)
        return op

    def op(self, engine, emit, reads=(), writes=()):
        return self._record(engine, emit, reads, writes, False)

    def dma(self, queue, emit, reads=(), writes=()):
        j = self.dma_count[queue]
        self.dma_count[queue] = j + 1
        slot = j % N_DMA_SLOTS
        target = 16 * (j // N_DMA_SLOTS + 1)
        op = self._record(queue, emit, reads, writes, True, slot, target)
        if j >= N_DMA_SLOTS:
            op["deps"].append({
                "engine": queue, "is_dma": True, "dma_slot": slot,
                "dma_target": target - 16, "flag": False, "_sentinel": True,
            })
        return op

    def barrier(self):
        deps = []
        for e in ("tensor", "scalar", "vector"):
            if self.ops[e]:
                last = self.ops[e][-1]
                last["flag"] = True
                deps.append(last)
        for q in DMA_QUEUES:
            slot_max = {}
            for o in self.ops[q]:
                if o["is_dma"]:
                    slot_max[o["dma_slot"]] = o
            deps.extend(slot_max.values())
        for d in deps:
            d["bar"] = True
        for e in ENGINES:
            self.pending_barrier[e] = list(deps)

    def finalize_values(self):
        for e in ENGINES:
            v = 0
            for op in self.ops[e]:
                if op["flag"] and not op["is_dma"]:
                    v += 1
                    op["value"] = v

    def emit_engine(self, engine, e):
        watermark = {}
        for op in self.ops[engine]:
            for d in op["deps"]:
                if d["is_dma"]:
                    sem = self.dsem[d["engine"]][d["dma_slot"]]
                    tgt = d["dma_target"]
                else:
                    sem = self.psem[d["engine"]]
                    tgt = d["value"]
                    assert tgt is not None
                k = id(sem)
                if watermark.get(k, -1) >= tgt:
                    continue
                watermark[k] = tgt
                e.wait_ge(sem, tgt)
            inst = op["emit"](e)
            if op["is_dma"]:
                inst.then_inc(self.dsem[engine][op["dma_slot"]], 16)
            elif op["flag"]:
                inst.then_inc(self.psem[engine], 1)

    def final_wait(self, engine, e, targets):
        for q in DMA_QUEUES:
            slot_max = {}
            for o in self.ops[q]:
                if o["is_dma"]:
                    slot_max[o["dma_slot"]] = o["dma_target"]
            for slot, tgt in sorted(slot_max.items()):
                e.wait_ge(self.dsem[q][slot], tgt)


def build_megakernel(T_main=4096, T_pref=4096, NB=1024, NT=256):
    assert T_main % NT == 0 and T_pref % NT == 0
    assert T_main % NB == 0 and NB % 128 == 0
    NSUB = 512 if NB >= 512 else NB
    n_ns = NB // NSUB
    D, F = 1024, 4096
    NDT = D // 128
    NFC = F // 128
    NEXP = 9
    n_nb = T_main // NB
    nt_pref = T_pref // NT
    nt_main = T_main // NT
    NPT = NT // 128
    NG = NT // 128

    nc = bass.Bass("TRN2", target_bir_lowering=False)

    x_main = nc.dram_tensor("x_main", [T_main, D], F32, kind="ExternalInput").ap()
    x_pref = nc.dram_tensor("x_pref", [max(T_pref, 1), D], F32, kind="ExternalInput").ap()
    mask_in = nc.dram_tensor("mask", [128, 1], F32, kind="ExternalInput").ap()
    WrT = nc.dram_tensor("WrT", [D, D], F32, kind="ExternalInput").ap()
    WkT = nc.dram_tensor("WkT", [D, D], F32, kind="ExternalInput").ap()
    WvT = nc.dram_tensor("WvT", [D, D], F32, kind="ExternalInput").ap()
    WwT = nc.dram_tensor("WwT", [D, D], F32, kind="ExternalInput").ap()
    WoT = nc.dram_tensor("WoT", [D, D], F32, kind="ExternalInput").ap()
    gWT = nc.dram_tensor("gWT", [D, 8], F32, kind="ExternalInput").ap()
    Wst1 = nc.dram_tensor("Wst1", [NEXP * NFC, 128, 2048], F32, kind="ExternalInput").ap()
    Wst2 = nc.dram_tensor("Wst2", [NEXP * NDT, 128, 4096], F32, kind="ExternalInput").ap()
    kbE = nc.dram_tensor("kbE", [16, D], BF16, kind="ExternalInput").ap()
    idn_in = nc.dram_tensor("idn", [128, 128], F32, kind="ExternalInput").ap()
    idnb_in = nc.dram_tensor("idnb", [128, 128], BF16, kind="ExternalInput").ap()
    ones_col_in = nc.dram_tensor("ones_col", [128, 1], F32, kind="ExternalInput").ap()
    ones_colb_in = nc.dram_tensor("ones_colb", [128, 1], BF16, kind="ExternalInput").ap()
    ones_row_in = nc.dram_tensor("ones_row", [1, 128], F32, kind="ExternalInput").ap()
    out_t = nc.dram_tensor("out", [T_main, D], F32, kind="ExternalOutput").ap()

    hscr = nc.dram_tensor("hscr", [D, T_main], F32, kind="Internal").ap()
    hscr_bf = nc.dram_tensor("hscr_bf", [D, T_main], BF16, kind="Internal").ap()
    gscr = nc.dram_tensor("gscr", [8, T_main], F32, kind="Internal").ap()
    wbs1 = nc.dram_tensor("wbs1", [NEXP * NFC, 128, 2048], BF16, kind="Internal").ap()
    wbs2 = nc.dram_tensor("wbs2", [NEXP * NDT, 128, 4096], BF16, kind="Internal").ap()

    out_stores = []

    with contextlib.ExitStack() as ctx:
        P = Prog(nc, ctx)

        def sb(name, shape, dtype, stack=None):
            return (stack or ctx).enter_context(nc.sbuf_tensor(name, shape, dtype))

        def ps(name, stack=None):
            return (stack or ctx).enter_context(nc.psum_tensor(name, [128, 512], F32))

        idn = sb("idn_s", [128, 128], F32)
        idnb = sb("idnb_s", [128, 128], BF16)
        ones_col = sb("ones_col_s", [128, 1], F32)
        ones_colb = sb("ones_colb_s", [128, 1], BF16)
        ones_row = sb("ones_row_s", [1, 128], F32)
        mask_s = sb("mask_s", [128, 1], F32)
        gwts = sb("gwts", [128, 64], F32)
        minit = sb("minit", [128, NDT], F32)
        consts = sb("consts", [128, 2], F32)

        P.op("vector", lambda e: e.memset(consts[:, 0:1], EPS_RMS), writes=["consts"])
        P.op("vector", lambda e: e.memset(consts[:, 1:2], EPS_LN), writes=["consts"])
        P.dma("sync", lambda e: e.dma_start(out=idn[:], in_=idn_in), writes=["idn"])
        P.dma("sync", lambda e: e.dma_start(out=idnb[:], in_=idnb_in), writes=["idnb"])
        P.dma("sync", lambda e: e.dma_start(out=ones_col[:], in_=ones_col_in), writes=["ones_col"])
        P.dma("sync", lambda e: e.dma_start(out=ones_colb[:], in_=ones_colb_in), writes=["ones_colb"])
        P.dma("sync", lambda e: e.dma_start(out=ones_row[:], in_=ones_row_in), writes=["ones_row"])
        P.dma("sync", lambda e: e.dma_start(out=mask_s[:], in_=mask_in), writes=["mask"])
        P.dma("sync", lambda e: e.dma_start(
            out=gwts[:].rearrange("p (kt e) -> p kt e", kt=8),
            in_=gWT.rearrange("(kt kp) e -> kp kt e", kp=128)),
            writes=["gwts"])

        with contextlib.ExitStack() as actx:
            wv_bf = sb("wv_bf", [128, 8 * D], BF16, actx)
            ww_bf = sb("ww_bf", [128, 8 * D], BF16, actx)
            wr_bf = sb("wr_bf", [128, 8 * D], BF16, actx)
            wo_bf = sb("wo_bf", [128, 8 * D], BF16, actx)
            wkts = sb("wkts", [128, 8 * 16], BF16, actx)
            kbE_s = sb("kbE_s", [16, D], BF16, actx)
            xs = [sb(f"xs{i}", [128, NPT * D], F32, actx) for i in range(3)]
            xs_bf = [sb(f"xsb{i}", [128, NPT * D], BF16, actx) for i in range(2)]
            xT = sb("xT", [128, NDT * NT], F32, actx)
            xTb = [sb(f"xTb{i}", [128, NDT * NT], BF16, actx) for i in range(2)]
            u_t = sb("u_t", [128, NDT * NT], F32, actx)
            w_t = sb("w_t", [128, NDT * NT], F32, actx)
            r_bf = [sb(f"r_bf{i}", [128, NDT * NT], BF16, actx) for i in range(2)]
            s_t = [sb(f"s_t{i}", [128, NDT * NT], F32, actx) for i in range(2)]
            out_f = sb("out_f", [128, NDT * NT], F32, actx)
            sq_t = sb("sq_t", [128, NDT * NT], BF16, actx)
            nrm_bf = sb("nrm_bf", [128, NDT * NT], BF16, actx)
            h_t = [sb(f"h_t{i}", [128, NDT * NT], F32, actx) for i in range(2)]
            hbfs = [sb(f"hbfs{i}", [128, NDT * NT], BF16, actx) for i in range(2)]
            kb_sb = [sb(f"kb_sb{i}", [128, NT], F32, actx) for i in range(2)]
            ksum_sb = sb("ksum_sb", [16, NT], BF16, actx)
            srow = sb("srow", [1, NT * 4], F32, actx)
            lgrow = [sb(f"lgrow{i}", [8, NT], F32, actx) for i in range(2)]
            ltok = sb("ltok", [128, 8], F32, actx)
            gtok = sb("gtok", [128, 8], F32, actx)
            gfm_t = sb("gfm_t", [8, NT], F32, actx)
            gcol = sb("gcol", [128, 8], F32, actx)
            ngsum = sb("ngsum", [8, 1], F32, actx)
            ngs8 = sb("ngs8", [128, 8], F32, actx)
            lt10 = sb("lt10", [128, 16], F32, actx)
            gtok_r = [sb(f"gtokr{i}", [128, 8 * NG], F32, actx) for i in range(2)]
            rrms_sb = sb("rrms_sb", [128, NT], F32, actx)

            pbank = [ps(f"pb{i}", actx) for i in range(3)]
            ptr = ps("ptr", actx)
            pks = actx.enter_context(nc.psum_tensor("pks", [128, 1024], BF16))
            pks2 = actx.enter_context(nc.psum_tensor("pks2", [128, 1024], BF16))
            pst = ps("pst", actx)
            pbc = ps("pbc", actx)

            bank_seq = [0]

            def next_bank():
                b = pbank[bank_seq[0] % 3]
                key = f"pbank{bank_seq[0] % 3}"
                bank_seq[0] += 1
                return b, key

            for nm, dst, srcw in (("wv", wv_bf, WvT), ("ww", ww_bf, WwT),
                                  ("wr", wr_bf, WrT), ("wo", wo_bf, WoT)):
                P.dma("gpsimd", lambda e, dst=dst, srcw=srcw: e.dma_start(
                    out=dst[:].rearrange("p (kt o) -> p kt o", kt=8),
                    in_=srcw.rearrange("(kt kp) o -> kp kt o", kp=128)),
                    writes=[nm])
            P.dma("sync", lambda e: e.dma_start(out=kbE_s[:], in_=kbE), writes=["kbE"])

            for half in range(2):
                P.dma("sync", lambda e, half=half: e.dma_start(
                    out=xs[0][:].rearrange("p (kt o) -> p kt o", kt=2),
                    in_=WkT.rearrange("(h kt kp) o -> h kp kt o", h=2, kp=128)[half][:, 0:2]),
                    writes=[("xs", 0)])
                P.dma("sync", lambda e, half=half: e.dma_start(
                    out=xs[1][:].rearrange("p (kt o) -> p kt o", kt=2),
                    in_=WkT.rearrange("(h kt kp) o -> h kp kt o", h=2, kp=128)[half][:, 2:4]),
                    writes=[("xs", 1)])
                for kt2 in range(4):
                    t = xs[kt2 // 2]
                    off = (kt2 % 2) * D
                    kt = half * 4 + kt2
                    for h in range(16):
                        def _rs(e, t=t, off=off, kt=kt, h=h):
                            with nc.allow_low_precision(reason="bf16 matmul operand (head-summed Wk)"):
                                return e.reduce_sum(
                                    wkts[:, kt * 16 + h: kt * 16 + h + 1],
                                    t[:, off + h * 64: off + (h + 1) * 64], axis=AX.X)
                        P.op("vector", _rs, reads=[("xs", kt2 // 2)], writes=["wkts"])

            for kt in range(8):
                P.op("tensor", lambda e, kt=kt: e.matmul(
                    pst[0:1, 0:8], ones_col[:], gwts[:, kt * 8:(kt + 1) * 8],
                    start=(kt == 0), stop=(kt == 7)),
                    reads=["gwts", "ones_col"], writes=["pst"])
            P.op("scalar", lambda e: e.copy(srow[0:1, 0:8], pst[0:1, 0:8]),
                 reads=["pst"], writes=["srow_gs"])
            P.op("tensor", lambda e: e.transpose(
                pbc[0:8, 0:1], srow[0:1, 0:8], idn[0:1, 0:1]),
                reads=["srow_gs", "idn"], writes=["pbc"])
            P.op("scalar", lambda e: e.mul(ngsum[:], pbc[0:8, 0:1], -1.0),
                 reads=["pbc"], writes=["ngsum"])
            P.op("scalar", lambda e: e.mul(srow[0:1, 8:16], srow[0:1, 0:8], -1.0),
                 reads=["srow_gs"], writes=["srow_ngs"])
            P.op("tensor", lambda e: e.matmul(
                pbc[:, 0:8], ones_row[:], srow[0:1, 8:16], start=True, stop=True),
                reads=["srow_ngs", "ones_row"], writes=["pbc"])
            P.op("vector", lambda e: e.tensor_copy(ngs8[:], pbc[:, 0:8]),
                 reads=["pbc"], writes=["ngs8"])

            def attn_frontA(i, nt_idx, is_pref, xsrc):
                sl = i % 2
                P.dma("gpsimd", lambda e: e.dma_start(
                    out=xs_bf[sl][:].rearrange("p (pt d) -> p pt d", pt=NPT),
                    in_=xsrc.rearrange("(n pt p) d -> n p pt d", p=128, pt=NPT)[nt_idx]),
                    writes=[("xsb", sl)])
                if not is_pref:
                    P.dma("sync", lambda e: e.dma_start(
                        out=xs[i % 3][:].rearrange("p (pt d) -> p pt d", pt=NPT),
                        in_=xsrc.rearrange("(n pt p) d -> n p pt d", p=128, pt=NPT)[nt_idx]),
                        writes=[("xs", i % 3)])
                for pt in range(NPT):
                    tb, tbkey = (pks, "pks") if pt % 2 == 0 else (pks2, "pks2")
                    for dt in range(NDT):
                        P.op("tensor", lambda e, pt=pt, dt=dt, tb=tb, sl=sl: e.transpose(
                            tb[:, dt * 128:(dt + 1) * 128],
                            xs_bf[sl][:, pt * D + dt * 128: pt * D + (dt + 1) * 128],
                            idnb[:]),
                            reads=[("xsb", sl), "idnb"], writes=[tbkey])
                    P.op("vector", lambda e, pt=pt, tb=tb, sl=sl: e.tensor_copy(
                        xTb[sl].rearrange("p (dt t) -> p dt t", dt=NDT)
                        [:, :, pt * 128:(pt + 1) * 128],
                        tb[:].rearrange("p (dt t) -> p dt t", dt=NDT)),
                        reads=[tbkey], writes=[("xTb", sl)])

            def attn_frontB(i, nt_idx, is_pref, tot):
                sl = i % 2
                for kt in range(NDT):
                    P.op("tensor", lambda e, kt=kt: e.matmul(
                        ptr[0:16, 0:NT], wkts[:, kt * 16:(kt + 1) * 16],
                        xTb[sl][:, kt * NT:(kt + 1) * NT],
                        start=(kt == 0), stop=(kt == NDT - 1)),
                        reads=["wkts", ("xTb", sl)],
                        writes=["ptr"])
                P.op("scalar", lambda e: e.copy(ksum_sb[:], ptr[0:16, 0:NT]),
                     reads=["ptr"], writes=["ksum_sb"])

                for dt in range(NDT):
                    bk, bkey = next_bank()
                    for kt in range(NDT):
                        P.op("tensor", lambda e, dt=dt, kt=kt, bk=bk: e.matmul(
                            bk[:, 0:NT], wv_bf[:, kt * D + dt * 128: kt * D + (dt + 1) * 128],
                            xTb[sl][:, kt * NT:(kt + 1) * NT],
                            start=(kt == 0), stop=(kt == NDT - 1)),
                            reads=["wv", ("xTb", sl)], writes=[bkey])
                    P.op("tensor", lambda e, dt=dt: e.matmul(
                        ptr[:, 0:NT], kbE_s[:, dt * 128:(dt + 1) * 128],
                        ksum_sb[:], start=True, stop=True),
                        reads=["kbE", "ksum_sb"], writes=["ptr"])
                    P.op("scalar", lambda e, dt=dt: e.copy(
                        kb_sb[dt % 2][:], ptr[:, 0:NT]),
                        reads=["ptr"], writes=[("kb", dt % 2)])
                    P.op("vector", lambda e, dt=dt, bk=bk: e.tensor_mul(
                        u_t[:, dt * NT:(dt + 1) * NT], bk[:, 0:NT], kb_sb[dt % 2][:]),
                        reads=[bkey, ("kb", dt % 2)], writes=[("u", dt)])
                for dt in range(NDT):
                    bk, bkey = next_bank()
                    for kt in range(NDT):
                        P.op("tensor", lambda e, dt=dt, kt=kt, bk=bk: e.matmul(
                            bk[:, 0:NT], ww_bf[:, kt * D + dt * 128: kt * D + (dt + 1) * 128],
                            xTb[sl][:, kt * NT:(kt + 1) * NT],
                            start=(kt == 0), stop=(kt == NDT - 1)),
                            reads=["ww", ("xTb", sl)], writes=[bkey])
                    P.op("scalar", lambda e, dt=dt, bk=bk: e.activation(
                        w_t[:, dt * NT:(dt + 1) * NT], bk[:, 0:NT], AF.Sigmoid),
                        reads=[bkey], writes=[("wsg", dt)])
                    P.op("vector", lambda e, dt=dt: e.tensor_scalar_mul(
                        w_t[:, dt * NT:(dt + 1) * NT],
                        w_t[:, dt * NT:(dt + 1) * NT], 0.99),
                        reads=[("wsg", dt)], writes=[("w", dt)])
                if not is_pref:
                    for dt in range(NDT):
                        bk, bkey = next_bank()
                        for kt in range(NDT):
                            P.op("tensor", lambda e, dt=dt, kt=kt, bk=bk: e.matmul(
                                bk[:, 0:NT], wr_bf[:, kt * D + dt * 128: kt * D + (dt + 1) * 128],
                                xTb[sl][:, kt * NT:(kt + 1) * NT],
                                start=(kt == 0), stop=(kt == NDT - 1)),
                                reads=["wr", ("xTb", sl)], writes=[bkey])
                        P.op("scalar", lambda e, dt=dt, bk=bk: e.activation(
                            r_bf[sl][:, dt * NT:(dt + 1) * NT], bk[:, 0:NT], AF.Sigmoid),
                            reads=[bkey], writes=[("r", sl, dt)])

                first_main = (not is_pref) and nt_idx == 0
                for dt in range(NDT):
                    if tot == 0:
                        init = 0.0
                        init_reads = []
                    elif first_main:
                        init = minit[:, dt:dt + 1]
                        init_reads = [("minit",)]
                    else:
                        psl = (i - 1) % 2
                        init = s_t[psl][:, dt * NT + NT - 1: dt * NT + NT]
                        init_reads = [("s", psl, dt)]
                    P.op("vector", lambda e, dt=dt, init=init: e.tensor_tensor_scan(
                        s_t[sl][:, dt * NT:(dt + 1) * NT],
                        w_t[:, dt * NT:(dt + 1) * NT],
                        u_t[:, dt * NT:(dt + 1) * NT],
                        init, ALU.mult, ALU.add),
                        reads=[("w", dt), ("u", dt)] + init_reads,
                        writes=[("s", sl, dt)])

                if is_pref and nt_idx == nt_pref - 1:
                    P.op("vector", lambda e: e.tensor_scalar(
                        minit[:],
                        s_t[sl].rearrange("p (dt t) -> p dt t", t=NT)[:, :, NT - 1:NT]
                        .rearrange("p dt one -> p (dt one)"),
                        mask_s[:], None, ALU.mult),
                        reads=[("s", sl, dt2) for dt2 in range(NDT)] + ["mask"],
                        writes=["minit"])

            def attn_back(i, nt_idx):
                sl = i % 2
                t0 = nt_idx * NT
                for pt in range(NPT):
                    tb, tbkey = (pst, "pst") if pt % 2 == 0 else (pbc, "pbc")
                    for dt4 in range(NDT // 4):
                        for j in range(4):
                            dt = dt4 * 4 + j
                            P.op("tensor", lambda e, pt=pt, dt=dt, j=j, tb=tb, i=i: e.transpose(
                                tb[:, j * 128:(j + 1) * 128],
                                xs[i % 3][:, pt * D + dt * 128: pt * D + (dt + 1) * 128],
                                idn[:]),
                                reads=[("xs", i % 3), "idn"], writes=[tbkey])
                        P.op("vector", lambda e, pt=pt, dt4=dt4, tb=tb: e.tensor_copy(
                            xT.rearrange("p (dt t) -> p dt t", dt=NDT)
                            [:, dt4 * 4:(dt4 + 1) * 4, pt * 128:(pt + 1) * 128],
                            tb[:, 0:512].rearrange("p (j t) -> p j t", j=4)),
                            reads=[tbkey], writes=[("xT", dt4)])
                xT_reads = [("xT", dt4) for dt4 in range(NDT // 4)]
                for dt in range(NDT):
                    P.op("vector", lambda e, dt=dt: e.tensor_mul(
                        out_f[:, dt * NT:(dt + 1) * NT],
                        s_t[sl][:, dt * NT:(dt + 1) * NT],
                        r_bf[sl][:, dt * NT:(dt + 1) * NT]),
                        reads=[("s", sl, dt), ("r", sl, dt)], writes=[("outf", dt)])
                    P.op("scalar", lambda e, dt=dt: e.activation(
                        sq_t[:, dt * NT:(dt + 1) * NT],
                        out_f[:, dt * NT:(dt + 1) * NT], AF.Square),
                        reads=[("outf", dt)], writes=[("sq", dt)])
                for dt in range(NDT):
                    P.op("tensor", lambda e, dt=dt: e.matmul(
                        pst[0:1, 0:NT], ones_colb[:], sq_t[:, dt * NT:(dt + 1) * NT],
                        start=(dt == 0), stop=(dt == NDT - 1)),
                        reads=[("sq", dt), "ones_colb"], writes=["pst"])
                P.op("scalar", lambda e: e.activation(
                    srow[0:1, 0:NT], pst[0:1, 0:NT], AF.Sqrt,
                    bias=consts[0:1, 0:1], scale=1.0 / D),
                    reads=["pst", "consts"], writes=["srow_rms0"])
                P.op("vector", lambda e: e.reciprocal(
                    srow[0:1, NT:2 * NT], srow[0:1, 0:NT]),
                    reads=["srow_rms0"], writes=["srow_rms1"])
                P.op("scalar", lambda e: e.copy(nrm_bf[:], out_f[:]),
                     reads=[("outf", dt2) for dt2 in range(NDT)], writes=["nrm"])
                wo_banks = []
                for dt in range(NDT):
                    bk, bkey = next_bank()
                    wo_banks.append((bk, bkey))
                    for kt in range(NDT):
                        P.op("tensor", lambda e, dt=dt, kt=kt, bk=bk: e.matmul(
                            bk[:, 0:NT], wo_bf[:, kt * D + dt * 128: kt * D + (dt + 1) * 128],
                            nrm_bf[:, kt * NT:(kt + 1) * NT],
                            start=(kt == 0), stop=(kt == NDT - 1)),
                            reads=["wo", "nrm"],
                            writes=[bkey])
                    if dt == 1:
                        P.op("tensor", lambda e: e.matmul(
                            pbc[:, 0:NT], ones_row[:], srow[0:1, NT:2 * NT],
                            start=True, stop=True),
                            reads=["srow_rms1", "ones_row"], writes=["pbc"])
                        P.op("vector", lambda e: e.tensor_copy(
                            rrms_sb[:], pbc[:, 0:NT]),
                            reads=["pbc"], writes=["rrms"])
                    if dt >= 1:
                        pd = dt - 1
                        pbk, pbkey = wo_banks[pd]
                        P.op("vector", lambda e, pd=pd, pbk=pbk: e.tensor_mul(
                            out_f[:, pd * NT:(pd + 1) * NT], pbk[:, 0:NT], rrms_sb[:]),
                            reads=[pbkey, "rrms"], writes=[("outf", pd)])
                        P.op("vector", lambda e, pd=pd: e.tensor_add(
                            out_f[:, pd * NT:(pd + 1) * NT],
                            out_f[:, pd * NT:(pd + 1) * NT],
                            xT[:, pd * NT:(pd + 1) * NT]),
                            reads=[("outf", pd)] + xT_reads, writes=[("outf", pd)])
                        P.op("scalar", lambda e, pd=pd: e.activation(
                            sq_t[:, pd * NT:(pd + 1) * NT],
                            out_f[:, pd * NT:(pd + 1) * NT], AF.Square),
                            reads=[("outf", pd)], writes=[("sq", pd)])
                pd = NDT - 1
                pbk, pbkey = wo_banks[pd]
                P.op("vector", lambda e, pd=pd, pbk=pbk: e.tensor_mul(
                    out_f[:, pd * NT:(pd + 1) * NT], pbk[:, 0:NT], rrms_sb[:]),
                    reads=[pbkey, "rrms"], writes=[("outf", pd)])
                P.op("vector", lambda e, pd=pd: e.tensor_add(
                    out_f[:, pd * NT:(pd + 1) * NT],
                    out_f[:, pd * NT:(pd + 1) * NT],
                    xT[:, pd * NT:(pd + 1) * NT]),
                    reads=[("outf", pd)] + xT_reads, writes=[("outf", pd)])
                P.op("scalar", lambda e, pd=pd: e.activation(
                    sq_t[:, pd * NT:(pd + 1) * NT],
                    out_f[:, pd * NT:(pd + 1) * NT], AF.Square),
                    reads=[("outf", pd)], writes=[("sq", pd)])
                for dt in range(NDT):
                    P.op("tensor", lambda e, dt=dt: e.matmul(
                        pst[0:1, 0:NT], ones_col[:], out_f[:, dt * NT:(dt + 1) * NT],
                        start=(dt == 0), stop=(dt == NDT - 1)),
                        reads=[("outf", dt), "ones_col"], writes=["pst"])
                for dt in range(NDT):
                    P.op("tensor", lambda e, dt=dt: e.matmul(
                        pst[0:1, NT:2 * NT], ones_colb[:], sq_t[:, dt * NT:(dt + 1) * NT],
                        start=(dt == 0), stop=(dt == NDT - 1)),
                        reads=[("sq", dt), "ones_colb"], writes=["pst"])
                for kt in range(NDT):
                    P.op("tensor", lambda e, kt=kt: e.matmul(
                        ptr[0:8, 0:NT], gwts[:, kt * 8:(kt + 1) * 8],
                        out_f[:, kt * NT:(kt + 1) * NT],
                        start=(kt == 0), stop=(kt == NDT - 1)),
                        reads=[("outf", kt), "gwts"], writes=["ptr"])
                P.op("scalar", lambda e: e.mul(srow[0:1, 0:NT], pst[0:1, 0:NT], 1.0 / D),
                     reads=["pst"], writes=["srow_m"])
                P.op("scalar", lambda e: e.activation(
                    srow[0:1, NT:2 * NT], srow[0:1, 0:NT], AF.Square),
                    reads=["srow_m"], writes=["srow_m2"])
                P.op("scalar", lambda e: e.mul(srow[0:1, 2 * NT:3 * NT], pst[0:1, NT:2 * NT], 1.0 / D),
                     reads=["pst"], writes=["srow_q"])
                P.op("scalar", lambda e, sl=sl: e.copy(lgrow[sl][:], ptr[0:8, 0:NT]),
                     reads=["ptr"], writes=[("lgrow", sl)])
                P.op("vector", lambda e: e.tensor_sub(
                    srow[0:1, 2 * NT:3 * NT], srow[0:1, 2 * NT:3 * NT], srow[0:1, NT:2 * NT]),
                    reads=["srow_q", "srow_m2"], writes=["srow_var"])
                P.op("scalar", lambda e: e.activation(
                    srow[0:1, 3 * NT:4 * NT], srow[0:1, 2 * NT:3 * NT], AF.Sqrt,
                    bias=consts[0:1, 1:2]),
                    reads=["srow_var", "consts"], writes=["srow_std"])
                P.op("vector", lambda e: e.reciprocal(
                    srow[0:1, 2 * NT:3 * NT], srow[0:1, 3 * NT:4 * NT]),
                    reads=["srow_std"], writes=["srow_rstd"])
                P.op("tensor", lambda e: e.matmul(
                    pbc[:, 0:NT], ones_row[:], srow[0:1, 0:NT], start=True, stop=True),
                    reads=["srow_m", "ones_row"], writes=["pbc"])
                P.op("tensor", lambda e: e.matmul(
                    pst[:, 0:NT], ones_row[:], srow[0:1, 2 * NT:3 * NT], start=True, stop=True),
                    reads=["srow_rstd", "ones_row", "pst"], writes=["pst"])
                for dt in range(NDT):
                    P.op("vector", lambda e, dt=dt: e.tensor_sub(
                        h_t[sl][:, dt * NT:(dt + 1) * NT],
                        out_f[:, dt * NT:(dt + 1) * NT], pbc[:, 0:NT]),
                        reads=[("outf", dt), "pbc"], writes=[("h", sl, dt)])
                for dt in range(NDT):
                    P.op("vector", lambda e, dt=dt: e.tensor_mul(
                        h_t[sl][:, dt * NT:(dt + 1) * NT],
                        h_t[sl][:, dt * NT:(dt + 1) * NT], pst[:, 0:NT]),
                        reads=[("h", sl, dt), "pst"], writes=[("h", sl, dt)])
                h_reads = [("h", sl, dt2) for dt2 in range(NDT)]
                P.dma("sync", lambda e, t0=t0: e.dma_start(
                    out=hscr.rearrange("(dt kp) T -> kp dt T", kp=128)[:, :, t0:t0 + NT],
                    in_=h_t[sl][:].rearrange("p (dt t) -> p dt t", dt=NDT)),
                    reads=h_reads, writes=[("hscr", nt_idx)])
                P.op("scalar", lambda e: e.copy(hbfs[sl][:], h_t[sl][:]),
                     reads=h_reads, writes=[("hbfs", sl)])
                P.dma("sync", lambda e, t0=t0: e.dma_start(
                    out=hscr_bf.rearrange("(dt kp) T -> kp dt T", kp=128)[:, :, t0:t0 + NT],
                    in_=hbfs[sl][:].rearrange("p (dt t) -> p dt t", dt=NDT)),
                    reads=[("hbfs", sl)], writes=[("hscr_bf", nt_idx)])

                for g in range(NG):
                    P.op("tensor", lambda e, g=g, sl=sl: e.transpose(
                        pbc[:, 0:8], lgrow[sl][:, g * 128:(g + 1) * 128], idn[0:8, 0:8]),
                        reads=[("lgrow", sl), "idn"], writes=["pbc"])
                    P.op("tensor", lambda e, g=g: e.transpose(
                        pbc[:, 8:9], srow[0:1, g * 128:(g + 1) * 128], idn[0:1, 0:1]),
                        reads=["srow_m", "idn"], writes=["pbc"])
                    P.op("tensor", lambda e, g=g: e.transpose(
                        pbc[:, 9:10], srow[0:1, 2 * NT + g * 128: 2 * NT + (g + 1) * 128],
                        idn[0:1, 0:1]),
                        reads=["srow_rstd", "idn"], writes=["pbc"])
                    P.op("vector", lambda e: e.tensor_copy(lt10[:, 0:10], pbc[:, 0:10]),
                         reads=["pbc"], writes=["lt10"])
                    P.op("vector", lambda e: e.scalar_tensor_tensor(
                        ltok[:], ngs8[:], lt10[:, 8:9], lt10[:, 0:8], ALU.mult, ALU.add),
                        reads=["lt10", "ngs8"], writes=["ltok"])
                    P.op("vector", lambda e: e.tensor_scalar(
                        ltok[:], ltok[:], lt10[:, 9:10], None, ALU.mult),
                        reads=["ltok", "lt10"], writes=["ltok"])
                    P.op("vector", lambda e: e.reduce_max(gcol[:, 0:1], ltok[:], axis=AX.X),
                         reads=["ltok"], writes=["g_m1"])
                    P.op("vector", lambda e: e.tensor_scalar_mul(gcol[:, 1:2], gcol[:, 0:1], -1.0),
                         reads=["g_m1"], writes=["g_nm1"])
                    P.op("vector", lambda e: e.tensor_scalar(
                        gtok[:], ltok[:], gcol[:, 0:1], None, ALU.is_ge),
                        reads=["ltok", "g_m1"], writes=["gtok"])
                    P.op("vector", lambda e: e.scalar_tensor_tensor(
                        gtok[:], gtok[:], -1e30, ltok[:], ALU.mult, ALU.add),
                        reads=["gtok", "ltok"], writes=["gtok"])
                    P.op("vector", lambda e: e.reduce_max(gcol[:, 2:3], gtok[:], axis=AX.X),
                         reads=["gtok"], writes=["g_m2"])
                    P.op("scalar", lambda e: e.activation(
                        gtok[:], ltok[:], AF.Exp, bias=gcol[:, 1:2], accum_out=gcol[:, 3:4]),
                        reads=["ltok", "g_nm1"], writes=["gtok", "g_z"])
                    P.op("vector", lambda e: e.reciprocal(gcol[:, 4:5], gcol[:, 3:4]),
                         reads=["g_z"], writes=["g_rz"])
                    P.op("vector", lambda e: e.tensor_scalar(
                        ltok[:], ltok[:], gcol[:, 2:3], None, ALU.is_ge),
                        reads=["ltok", "g_m2"], writes=["ltok"])
                    P.op("vector", lambda e: e.tensor_mul(gtok[:], gtok[:], ltok[:]),
                         reads=["gtok", "ltok"], writes=["gtok"])
                    P.op("vector", lambda e, g=g, sl=sl: e.tensor_scalar(
                        gtok_r[sl][:, g * 8:(g + 1) * 8], gtok[:], gcol[:, 4:5], 0.5,
                        ALU.mult, ALU.mult),
                        reads=["gtok", "g_rz"], writes=[("gtokr", sl, g)])

            def attn_gates(i, nt_idx):
                sl = i % 2
                t0 = nt_idx * NT
                for g in range(NG):
                    P.op("tensor", lambda e, g=g, sl=sl: e.transpose(
                        ptr[0:8, 0:128], gtok_r[sl][:, g * 8:(g + 1) * 8], idn[:]),
                        reads=[("gtokr", sl, g), "idn"], writes=["ptr"])
                    P.op("scalar", lambda e, g=g: e.copy(
                        gfm_t[:, g * 128:(g + 1) * 128], ptr[0:8, 0:128]),
                        reads=["ptr"], writes=[("gfm", g)])
                P.dma("sync", lambda e, t0=t0: e.dma_start(
                    out=gscr[:, t0:t0 + NT], in_=gfm_t[:]),
                    reads=[("gfm", g) for g in range(NG)], writes=[("gscr", nt_idx)])

            tiles = [(ntp, True) for ntp in range(nt_pref)] + \
                    [(ntm, False) for ntm in range(nt_main)]
            stages = []
            for i, (nt_idx, is_pref) in enumerate(tiles):
                stages.append((i, nt_idx, is_pref))
            n_tiles = len(stages)
            for i in range(n_tiles + 3):
                if i < n_tiles:
                    ii, nt_idx, is_pref = stages[i]
                    attn_frontA(ii, nt_idx, is_pref, x_pref if is_pref else x_main)
                if 1 <= i <= n_tiles:
                    ii, nt_idx, is_pref = stages[i - 1]
                    attn_frontB(ii, nt_idx, is_pref, i - 1)
                if 2 <= i <= n_tiles + 1:
                    ii, nt_idx, is_pref = stages[i - 2]
                    if not is_pref:
                        attn_back(ii, nt_idx)
                if i >= 3:
                    ii, nt_idx, is_pref = stages[i - 3]
                    if not is_pref:
                        attn_gates(ii, nt_idx)

        P.barrier()

        with contextlib.ExitStack() as fctx:
            h_bfb = [sb(f"h_bfb{i}", [128, NDT * NB], BF16, fctx) for i in range(2)]
            h_res = [sb(f"h_res{i}", [128, NSUB], F32, fctx) for i in range(2)]
            gu = sb("gu", [128, NFC * NB], BF16, fctx)
            y_acc = sb("y_acc", [128, NDT * NB], F32, fctx)
            ws1 = [sb(f"ws1_{i}", [128, 2048], BF16, fctx) for i in range(4)]
            ws2 = [sb(f"ws2_{i}", [128, 4096], BF16, fctx) for i in range(2)]
            gbc = [sb(f"gbc{i}", [128, NSUB], F32, fctx) for i in range(2 * max(n_ns, 1))]
            sg_sb = [sb(f"sg{i}", [128, NSUB], BF16, fctx) for i in range(2)]
            ug_sb = [sb(f"ug{i}", [128, NSUB], BF16, fctx) for i in range(2)]
            ge_row = sb("ge_row", [1, NB], F32, fctx)
            r2tok = [sb(f"r2tok{i}", [128, D], F32, fctx) for i in range(2)]
            otok = [sb(f"otok{i}", [128, D], F32, fctx) for i in range(2)]
            ccol = sb("ccol", [128, 8], F32, fctx)

            psg = [ps(f"psg{i}", fctx) for i in range(2)]
            psu = [ps(f"psu{i}", fctx) for i in range(2)]
            psy = [ps(f"psy{i}", fctx) for i in range(2)]
            psx = ps("psx", fctx)
            psx2 = ps("psx2", fctx)

            def load_hbfb(nb2):
                b0x = nb2 * NB
                hbx = nb2 % 2
                P.dma("sync", lambda e, b0x=b0x, hbx=hbx: e.dma_start(
                    out=h_bfb[hbx][:].rearrange("p (dt t) -> p dt t", dt=NDT),
                    in_=hscr_bf.rearrange("(dt kp) T -> kp dt T", kp=128)[:, :, b0x:b0x + NB]),
                    reads=[("hscr_bf", q) for q in
                           range(nb2 * (NB // NT), (nb2 + 1) * (NB // NT))],
                    writes=[("h_bfb", hbx)])

            load_hbfb(0)
            for nb in range(n_nb):
                b0 = nb * NB
                hb = nb % 2
                hq = list(range(nb * (NB // NT), (nb + 1) * (NB // NT)))

                def emit_gbc(ee2, b0=b0, hq=hq):
                    P.dma("sync", lambda e, ee2=ee2, b0=b0: e.dma_start(
                        out=ge_row[:], in_=gscr[ee2:ee2 + 1, b0:b0 + NB]),
                        reads=[("gscr", q) for q in hq],
                        writes=["ge_row"])
                    sl2 = ee2 % 2
                    for ns in range(n_ns):
                        P.op("tensor", lambda e, ns=ns: e.matmul(
                            psx2[:, 0:NSUB], ones_row[:],
                            ge_row[0:1, ns * NSUB:(ns + 1) * NSUB],
                            start=True, stop=True),
                            reads=["ge_row", "ones_row"], writes=["psx2"])
                        P.op("vector", lambda e, ns=ns, sl2=sl2: e.tensor_copy(
                            gbc[sl2 * n_ns + ns][:], psx2[:, 0:NSUB]),
                            reads=["psx2"], writes=[("gbc", sl2, ns)])

                if nb == 0:
                    emit_gbc(0)
                for ee in range(NEXP):
                    for fc in range(NFC):
                        ci = ee * NFC + fc
                        slw = ci % 4
                        if fc == 2 and ee + 1 < 8:
                            emit_gbc(ee + 1)
                        if fc == 4 and ee == 4 and nb + 1 < n_nb:
                            load_hbfb(nb + 1)
                        if fc == 2 and ee == 8 and nb + 1 < n_nb:
                            emit_gbc(0, b0=(nb + 1) * NB,
                                     hq=list(range((nb + 1) * (NB // NT),
                                                   (nb + 2) * (NB // NT))))
                        if nb == 0:
                            P.dma("gpsimd", lambda e, ci=ci, slw=slw: e.dma_start(
                                out=ws1[slw][:], in_=Wst1[ci]),
                                writes=[("ws1", slw)])
                            P.dma("sync", lambda e, ci=ci, slw=slw: e.dma_start(
                                out=wbs1[ci], in_=ws1[slw][:]),
                                reads=[("ws1", slw)], writes=[("wbs1", ci)])
                        else:
                            P.dma("sync", lambda e, ci=ci, slw=slw: e.dma_start(
                                out=ws1[slw][:], in_=wbs1[ci]),
                                reads=[("wbs1", ci)], writes=[("ws1", slw)])
                        for ns in range(n_ns):
                            pgi = psg[ns % 2]
                            pui = psu[ns % 2]
                            for kt in range(NDT):
                                P.op("tensor", lambda e, ns=ns, kt=kt, slw=slw, pgi=pgi, hb=hb: e.matmul(
                                    pgi[:, 0:NSUB], ws1[slw][:, kt * 128:(kt + 1) * 128],
                                    h_bfb[hb][:, kt * NB + ns * NSUB: kt * NB + (ns + 1) * NSUB],
                                    start=(kt == 0), stop=(kt == NDT - 1)),
                                    reads=[("ws1", slw), ("h_bfb", hb)], writes=[f"psg{ns % 2}"])
                            for kt in range(NDT):
                                P.op("tensor", lambda e, ns=ns, kt=kt, slw=slw, pui=pui, hb=hb: e.matmul(
                                    pui[:, 0:NSUB], ws1[slw][:, 1024 + kt * 128: 1024 + (kt + 1) * 128],
                                    h_bfb[hb][:, kt * NB + ns * NSUB: kt * NB + (ns + 1) * NSUB],
                                    start=(kt == 0), stop=(kt == NDT - 1)),
                                    reads=[("ws1", slw), ("h_bfb", hb)], writes=[f"psu{ns % 2}"])
                            P.op("scalar", lambda e, ns=ns, pgi=pgi: e.activation(
                                sg_sb[ns % 2][:], pgi[:, 0:NSUB], AF.Silu),
                                reads=[f"psg{ns % 2}"], writes=[("sg", ns % 2)])
                            if ee < 8:
                                P.op("vector", lambda e, ns=ns, pui=pui, ee=ee: e.tensor_mul(
                                    ug_sb[ns % 2][:], pui[:, 0:NSUB],
                                    gbc[(ee % 2) * n_ns + ns][:]),
                                    reads=[f"psu{ns % 2}", ("gbc", ee % 2, ns)],
                                    writes=[("ug", ns % 2)])
                            else:
                                P.op("vector", lambda e, ns=ns, pui=pui: e.tensor_scalar(
                                    ug_sb[ns % 2][:], pui[:, 0:NSUB], 0.0625, None, ALU.mult),
                                    reads=[f"psu{ns % 2}"], writes=[("ug", ns % 2)])
                            P.op("vector", lambda e, fc=fc, ns=ns: e.tensor_mul(
                                gu[:, fc * NB + ns * NSUB: fc * NB + (ns + 1) * NSUB],
                                sg_sb[ns % 2][:], ug_sb[ns % 2][:]),
                                reads=[("sg", ns % 2), ("ug", ns % 2)],
                                writes=[("gu", fc, ns)])
                    for dt in range(NDT):
                        di = ee * NDT + dt
                        slw = di % 2
                        if nb == 0:
                            P.dma("gpsimd", lambda e, di=di, slw=slw: e.dma_start(
                                out=ws2[slw][:], in_=Wst2[di]),
                                writes=[("ws2", slw)])
                            P.dma("sync", lambda e, di=di, slw=slw: e.dma_start(
                                out=wbs2[di], in_=ws2[slw][:]),
                                reads=[("ws2", slw)], writes=[("wbs2", di)])
                        else:
                            P.dma("sync", lambda e, di=di, slw=slw: e.dma_start(
                                out=ws2[slw][:], in_=wbs2[di]),
                                reads=[("wbs2", di)], writes=[("ws2", slw)])
                        for ns in range(n_ns):
                            pyi = psy[ns % 2]
                            gu_reads = [("gu", fkt, ns) for fkt in range(NFC)]
                            for fkt in range(NFC):
                                P.op("tensor", lambda e, ns=ns, fkt=fkt, slw=slw, pyi=pyi: e.matmul(
                                    pyi[:, 0:NSUB], ws2[slw][:, fkt * 128:(fkt + 1) * 128],
                                    gu[:, fkt * NB + ns * NSUB: fkt * NB + (ns + 1) * NSUB],
                                    start=(fkt == 0), stop=(fkt == NFC - 1)),
                                    reads=[("ws2", slw)] + (gu_reads if fkt == 0 else [("gu", fkt, ns)]),
                                    writes=[f"psy{ns % 2}"])
                            if ee == 0:
                                P.op("vector", lambda e, dt=dt, ns=ns, pyi=pyi: e.tensor_copy(
                                    y_acc[:, dt * NB + ns * NSUB: dt * NB + (ns + 1) * NSUB],
                                    pyi[:, 0:NSUB]),
                                    reads=[f"psy{ns % 2}"], writes=[("y", dt, ns)])
                            else:
                                P.op("vector", lambda e, dt=dt, ns=ns, pyi=pyi: e.tensor_add(
                                    y_acc[:, dt * NB + ns * NSUB: dt * NB + (ns + 1) * NSUB],
                                    y_acc[:, dt * NB + ns * NSUB: dt * NB + (ns + 1) * NSUB],
                                    pyi[:, 0:NSUB]),
                                    reads=[f"psy{ns % 2}", ("y", dt, ns)],
                                    writes=[("y", dt, ns)])
                            if ee == NEXP - 1:
                                hs = (dt * n_ns + ns) % 2
                                P.dma("sync", lambda e, dt=dt, ns=ns, hs=hs, b0=b0: e.dma_start(
                                    out=h_res[hs][:],
                                    in_=hscr[dt * 128:(dt + 1) * 128,
                                             b0 + ns * NSUB: b0 + (ns + 1) * NSUB]),
                                    reads=[("hscr", q) for q in hq],
                                    writes=[("h_res", hs)])
                                P.op("vector", lambda e, dt=dt, ns=ns, hs=hs: e.tensor_add(
                                    y_acc[:, dt * NB + ns * NSUB: dt * NB + (ns + 1) * NSUB],
                                    y_acc[:, dt * NB + ns * NSUB: dt * NB + (ns + 1) * NSUB],
                                    h_res[hs][:]),
                                    reads=[("y", dt, ns), ("h_res", hs)], writes=[("y", dt, ns)])

                for tg in range(NB // 128):
                    stok = tg % 2
                    yreads = [("y", dt, (tg * 128) // NSUB) for dt in range(NDT)]
                    for half in range(2):
                        px, pxkey = (psx, "psx") if half == 0 else (psx2, "psx2")
                        for j in range(4):
                            dt = half * 4 + j
                            P.op("tensor", lambda e, dt=dt, j=j, tg=tg, px=px: e.transpose(
                                px[:, j * 128:(j + 1) * 128],
                                y_acc[:, dt * NB + tg * 128: dt * NB + (tg + 1) * 128],
                                idn[:]),
                                reads=yreads if (half == 0 and j == 0) else
                                [("y", dt, (tg * 128) // NSUB)],
                                writes=[pxkey])
                        P.op("vector", lambda e, half=half, stok=stok, px=px, pxkey=pxkey: e.tensor_copy(
                            r2tok[stok][:, half * 512:(half + 1) * 512], px[:, 0:512]),
                            reads=[pxkey], writes=[("r2", stok, half)])
                    r2r = [("r2", stok, 0), ("r2", stok, 1)]
                    P.op("vector", lambda e, stok=stok: e.reduce_sum(
                        ccol[:, 0:1], r2tok[stok][:], axis=AX.X),
                        reads=r2r, writes=["cc_s"])
                    P.op("scalar", lambda e, stok=stok: e.activation(
                        otok[stok][:], r2tok[stok][:], AF.Square, accum_out=ccol[:, 1:2]),
                        reads=r2r, writes=[("otok", stok), "cc_q"])
                    P.op("scalar", lambda e: e.mul(ccol[:, 2:3], ccol[:, 0:1], 1.0 / D),
                         reads=["cc_s"], writes=["cc_m"])
                    P.op("scalar", lambda e: e.activation(
                        ccol[:, 3:4], ccol[:, 2:3], AF.Square),
                        reads=["cc_m"], writes=["cc_m2"])
                    P.op("scalar", lambda e: e.mul(ccol[:, 4:5], ccol[:, 1:2], 1.0 / D),
                         reads=["cc_q"], writes=["cc_q2"])
                    P.op("vector", lambda e: e.tensor_sub(
                        ccol[:, 5:6], ccol[:, 4:5], ccol[:, 3:4]),
                        reads=["cc_q2", "cc_m2"], writes=["cc_var"])
                    P.op("scalar", lambda e: e.activation(
                        ccol[:, 6:7], ccol[:, 5:6], AF.Sqrt, bias=consts[:, 1:2]),
                        reads=["cc_var", "consts"], writes=["cc_std"])
                    P.op("vector", lambda e: e.reciprocal(ccol[:, 7:8], ccol[:, 6:7]),
                         reads=["cc_std"], writes=["cc_rstd"])
                    P.op("vector", lambda e, stok=stok: e.tensor_scalar(
                        otok[stok][:], r2tok[stok][:], ccol[:, 2:3], ccol[:, 7:8],
                        ALU.subtract, ALU.mult),
                        reads=r2r + ["cc_m", "cc_rstd"], writes=[("otok", stok)])
                    st = P.dma("gpsimd", lambda e, b0=b0, tg=tg, stok=stok: e.dma_start(
                        out=out_t[b0 + tg * 128: b0 + (tg + 1) * 128, :],
                        in_=otok[stok][:]),
                        reads=[("otok", stok)], writes=[("out", nb, tg)])
                    out_stores.append(st)

        P.finalize_values()
        with nc.Block() as block:
            @block.sync
            def _(e):
                P.emit_engine("sync", e)
                P.final_wait("sync", e, out_stores)

            @block.gpsimd
            def _(e):
                P.emit_engine("gpsimd", e)

            @block.tensor
            def _(e):
                P.emit_engine("tensor", e)

            @block.scalar
            def _(e):
                P.emit_engine("scalar", e)

            @block.vector
            def _(e):
                P.emit_engine("vector", e)

    return nc, "out"


def prep_inputs(inputs, T_main=4096, T_pref=4096):
    x = np.ascontiguousarray(inputs["L0_attn_x"])
    B, T, D = x.shape
    NFC, NDT, NEXP = 32, 8, 9

    def t(a):
        return np.ascontiguousarray(np.asarray(a).T)

    WrT, WkT, WvT, WwT, WoT = (t(inputs["L0_attn_Wr"]), t(inputs["L0_attn_Wk"]),
                               t(inputs["L0_attn_Wv"]), t(inputs["L0_attn_Ww"]),
                               t(inputs["L0_attn_Wo"]))
    gWT = t(inputs["L0_ffn_gate_W"])

    w1 = [np.asarray(inputs["L0_ffn_w1_e"][e]) for e in range(8)]
    w3 = [np.asarray(inputs["L0_ffn_w3_e"][e]) for e in range(8)]
    w2 = [np.asarray(inputs["L0_ffn_w2_e"][e]) for e in range(8)]
    w1.append(np.concatenate([inputs["L0_ffn_w1_s"][n] for n in range(8)], axis=1))
    w3.append(np.concatenate([inputs["L0_ffn_w3_s"][n] for n in range(8)], axis=1))
    w2.append(np.concatenate([inputs["L0_ffn_w2_s"][n] for n in range(8)], axis=0))

    st1 = np.empty((NEXP * NFC, 128, 2048), np.float32)
    st2 = np.empty((NEXP * NDT, 128, 4096), np.float32)
    for e in range(NEXP):
        a1 = w1[e].reshape(8, 128, NFC, 128).transpose(2, 1, 0, 3).reshape(NFC, 128, 1024)
        a3 = w3[e].reshape(8, 128, NFC, 128).transpose(2, 1, 0, 3).reshape(NFC, 128, 1024)
        st1[e * NFC:(e + 1) * NFC] = np.concatenate([a1, a3], axis=2)
        st2[e * NDT:(e + 1) * NDT] = (w2[e].reshape(NFC, 128, NDT, 128)
                                      .transpose(2, 1, 0, 3).reshape(NDT, 128, 4096))

    import ml_dtypes
    kbE = np.zeros((16, D), ml_dtypes.bfloat16)
    for c in range(D):
        kbE[(c % 128 >= 64) + 2 * (c // 128), c] = 1.0

    common = {
        "WrT": WrT, "WkT": WkT, "WvT": WvT, "WwT": WwT, "WoT": WoT,
        "gWT": np.ascontiguousarray(gWT), "Wst1": st1, "Wst2": st2, "kbE": kbE,
        "idn": np.eye(128, dtype=np.float32),
        "idnb": np.eye(128, dtype=ml_dtypes.bfloat16),
        "ones_col": np.ones((128, 1), np.float32),
        "ones_colb": np.ones((128, 1), ml_dtypes.bfloat16),
        "ones_row": np.ones((1, 128), np.float32),
    }
    n_half = T // T_main
    in_maps = []
    for core in range(B * n_half):
        b = core // n_half
        half = core % n_half
        m = dict(common)
        m["x_main"] = np.ascontiguousarray(x[b, half * T_main:(half + 1) * T_main])
        m["x_pref"] = np.ascontiguousarray(x[b, 0:max(T_pref, 1)])
        m["mask"] = np.full((128, 1), float(half > 0), np.float32)
        in_maps.append(m)
    return in_maps


def kernel(**inputs):
    T_main, T_pref = 4096, 4096
    x = inputs["L0_attn_x"]
    B, T, D = x.shape
    nc, out_name = build_megakernel(T_main=T_main, T_pref=T_pref)
    in_maps = prep_inputs(inputs, T_main=T_main, T_pref=T_pref)
    res = run_bass_kernel_spmd(nc, in_maps, core_ids=list(range(8)))
    out = np.empty((B, T, D), np.float32)
    n_half = T // T_main
    for core in range(8):
        b, half = core // n_half, core % n_half
        out[b, half * T_main:(half + 1) * T_main] = res.results[core][out_name]
    return out


# version 16
# speedup vs baseline: 1.1805x; 1.0007x over previous
import contextlib

import numpy as np
import concourse.bass as bass
import concourse.mybir as mybir
from concourse.bass_utils import run_bass_kernel_spmd

F32 = mybir.dt.float32
BF16 = mybir.dt.bfloat16
AF = mybir.ActivationFunctionType
ALU = mybir.AluOpType
AX = mybir.AxisListType

EPS_RMS = float(np.finfo(np.float32).eps)
EPS_LN = 1e-5

ENGINES = ("sync", "gpsimd", "tensor", "scalar", "vector")
DMA_QUEUES = ("sync", "gpsimd")
N_DMA_SLOTS = 8


class Prog:

    def __init__(self, nc, ctx):
        self.nc = nc
        self.ops = {e: [] for e in ENGINES}
        self.regions = {}
        self.psem = {e: ctx.enter_context(nc.semaphore(f"prog_{e}"))
                     for e in ("tensor", "scalar", "vector", "gpsimd")}
        self.dsem = {q: [ctx.enter_context(nc.semaphore(f"dma_{q}_{i}"))
                         for i in range(N_DMA_SLOTS)] for q in DMA_QUEUES}
        self.dma_count = {q: 0 for q in DMA_QUEUES}
        self.pending_barrier = {e: [] for e in ENGINES}

    def _region(self, key):
        r = self.regions.get(key)
        if r is None:
            r = {"w": None, "r": []}
            self.regions[key] = r
        return r

    def _record(self, engine, emit, reads, writes, is_dma, dma_slot=None,
                dma_target=None):
        op = {
            "engine": engine, "emit": emit, "idx": len(self.ops[engine]),
            "flag": False, "value": None, "deps": [],
            "is_dma": is_dma, "dma_slot": dma_slot, "dma_target": dma_target,
        }
        deps = list(self.pending_barrier[engine])
        self.pending_barrier[engine] = []
        for key in reads:
            r = self._region(key)
            if r["w"] is not None:
                deps.append(r["w"])
            r["r"].append(op)
        for key in writes:
            r = self._region(key)
            deps.extend(r["r"])
            if r["w"] is not None:
                deps.append(r["w"])
            r["w"] = op
            r["r"] = []
        seen = set()
        for d in deps:
            if d is op or id(d) in seen:
                continue
            seen.add(id(d))
            if d["engine"] == engine and not d["is_dma"] and not is_dma \
                    and engine == "tensor" and not d.get("bar"):
                continue
            if not d["is_dma"]:
                d["flag"] = True
            op["deps"].append(d)
        self.ops[engine].append(op)
        return op

    def op(self, engine, emit, reads=(), writes=()):
        return self._record(engine, emit, reads, writes, False)

    def dma(self, queue, emit, reads=(), writes=()):
        j = self.dma_count[queue]
        self.dma_count[queue] = j + 1
        slot = j % N_DMA_SLOTS
        target = 16 * (j // N_DMA_SLOTS + 1)
        op = self._record(queue, emit, reads, writes, True, slot, target)
        if j >= N_DMA_SLOTS:
            op["deps"].append({
                "engine": queue, "is_dma": True, "dma_slot": slot,
                "dma_target": target - 16, "flag": False, "_sentinel": True,
            })
        return op

    def barrier(self):
        deps = []
        for e in ("tensor", "scalar", "vector"):
            if self.ops[e]:
                last = self.ops[e][-1]
                last["flag"] = True
                deps.append(last)
        for q in DMA_QUEUES:
            slot_max = {}
            for o in self.ops[q]:
                if o["is_dma"]:
                    slot_max[o["dma_slot"]] = o
            deps.extend(slot_max.values())
        for d in deps:
            d["bar"] = True
        for e in ENGINES:
            self.pending_barrier[e] = list(deps)

    def finalize_values(self):
        for e in ENGINES:
            v = 0
            for op in self.ops[e]:
                if op["flag"] and not op["is_dma"]:
                    v += 1
                    op["value"] = v

    def emit_engine(self, engine, e):
        watermark = {}
        for op in self.ops[engine]:
            for d in op["deps"]:
                if d["is_dma"]:
                    sem = self.dsem[d["engine"]][d["dma_slot"]]
                    tgt = d["dma_target"]
                else:
                    sem = self.psem[d["engine"]]
                    tgt = d["value"]
                    assert tgt is not None
                k = id(sem)
                if watermark.get(k, -1) >= tgt:
                    continue
                watermark[k] = tgt
                e.wait_ge(sem, tgt)
            inst = op["emit"](e)
            if op["is_dma"]:
                inst.then_inc(self.dsem[engine][op["dma_slot"]], 16)
            elif op["flag"]:
                inst.then_inc(self.psem[engine], 1)

    def final_wait(self, engine, e, targets):
        for q in DMA_QUEUES:
            slot_max = {}
            for o in self.ops[q]:
                if o["is_dma"]:
                    slot_max[o["dma_slot"]] = o["dma_target"]
            for slot, tgt in sorted(slot_max.items()):
                e.wait_ge(self.dsem[q][slot], tgt)


def build_megakernel(T_main=4096, T_pref=4096, NB=1024, NT=256):
    assert T_main % NT == 0 and T_pref % NT == 0
    assert T_main % NB == 0 and NB % 128 == 0
    NSUB = 512 if NB >= 512 else NB
    n_ns = NB // NSUB
    D, F = 1024, 4096
    NDT = D // 128
    NFC = F // 128
    NEXP = 9
    n_nb = T_main // NB
    nt_pref = T_pref // NT
    nt_main = T_main // NT
    NPT = NT // 128
    NG = NT // 128

    nc = bass.Bass("TRN2", target_bir_lowering=False)

    x_main = nc.dram_tensor("x_main", [T_main, D], F32, kind="ExternalInput").ap()
    x_pref = nc.dram_tensor("x_pref", [max(T_pref, 1), D], F32, kind="ExternalInput").ap()
    mask_in = nc.dram_tensor("mask", [128, 1], F32, kind="ExternalInput").ap()
    WrT = nc.dram_tensor("WrT", [D, D], F32, kind="ExternalInput").ap()
    WkT = nc.dram_tensor("WkT", [D, D], F32, kind="ExternalInput").ap()
    WvT = nc.dram_tensor("WvT", [D, D], F32, kind="ExternalInput").ap()
    WwT = nc.dram_tensor("WwT", [D, D], F32, kind="ExternalInput").ap()
    WoT = nc.dram_tensor("WoT", [D, D], F32, kind="ExternalInput").ap()
    gWT = nc.dram_tensor("gWT", [D, 8], F32, kind="ExternalInput").ap()
    Wst1 = nc.dram_tensor("Wst1", [NEXP * NFC, 128, 2048], F32, kind="ExternalInput").ap()
    Wst2 = nc.dram_tensor("Wst2", [NEXP * NDT, 128, 4096], F32, kind="ExternalInput").ap()
    kbE = nc.dram_tensor("kbE", [16, D], BF16, kind="ExternalInput").ap()
    idn_in = nc.dram_tensor("idn", [128, 128], F32, kind="ExternalInput").ap()
    idnb_in = nc.dram_tensor("idnb", [128, 128], BF16, kind="ExternalInput").ap()
    ones_col_in = nc.dram_tensor("ones_col", [128, 1], F32, kind="ExternalInput").ap()
    ones_colb_in = nc.dram_tensor("ones_colb", [128, 1], BF16, kind="ExternalInput").ap()
    ones_row_in = nc.dram_tensor("ones_row", [1, 128], F32, kind="ExternalInput").ap()
    out_t = nc.dram_tensor("out", [T_main, D], F32, kind="ExternalOutput").ap()

    hscr = nc.dram_tensor("hscr", [D, T_main], F32, kind="Internal").ap()
    hscr_bf = nc.dram_tensor("hscr_bf", [D, T_main], BF16, kind="Internal").ap()
    gscr = nc.dram_tensor("gscr", [8, T_main], F32, kind="Internal").ap()
    wbs1 = nc.dram_tensor("wbs1", [NEXP * NFC, 128, 2048], BF16, kind="Internal").ap()
    wbs2 = nc.dram_tensor("wbs2", [NEXP * NDT, 128, 4096], BF16, kind="Internal").ap()

    out_stores = []

    with contextlib.ExitStack() as ctx:
        P = Prog(nc, ctx)

        def sb(name, shape, dtype, stack=None):
            return (stack or ctx).enter_context(nc.sbuf_tensor(name, shape, dtype))

        def ps(name, stack=None):
            return (stack or ctx).enter_context(nc.psum_tensor(name, [128, 512], F32))

        idn = sb("idn_s", [128, 128], F32)
        idnb = sb("idnb_s", [128, 128], BF16)
        ones_col = sb("ones_col_s", [128, 1], F32)
        ones_colb = sb("ones_colb_s", [128, 1], BF16)
        ones_row = sb("ones_row_s", [1, 128], F32)
        mask_s = sb("mask_s", [128, 1], F32)
        gwts = sb("gwts", [128, 64], F32)
        minit = sb("minit", [128, NDT], F32)
        consts = sb("consts", [128, 2], F32)

        P.op("vector", lambda e: e.memset(consts[:, 0:1], EPS_RMS), writes=["consts"])
        P.op("vector", lambda e: e.memset(consts[:, 1:2], EPS_LN), writes=["consts"])
        P.dma("sync", lambda e: e.dma_start(out=idn[:], in_=idn_in), writes=["idn"])
        P.dma("sync", lambda e: e.dma_start(out=idnb[:], in_=idnb_in), writes=["idnb"])
        P.dma("sync", lambda e: e.dma_start(out=ones_col[:], in_=ones_col_in), writes=["ones_col"])
        P.dma("sync", lambda e: e.dma_start(out=ones_colb[:], in_=ones_colb_in), writes=["ones_colb"])
        P.dma("sync", lambda e: e.dma_start(out=ones_row[:], in_=ones_row_in), writes=["ones_row"])
        P.dma("sync", lambda e: e.dma_start(out=mask_s[:], in_=mask_in), writes=["mask"])
        P.dma("sync", lambda e: e.dma_start(
            out=gwts[:].rearrange("p (kt e) -> p kt e", kt=8),
            in_=gWT.rearrange("(kt kp) e -> kp kt e", kp=128)),
            writes=["gwts"])

        with contextlib.ExitStack() as actx:
            wv_bf = sb("wv_bf", [128, 8 * D], BF16, actx)
            ww_bf = sb("ww_bf", [128, 8 * D], BF16, actx)
            wr_bf = sb("wr_bf", [128, 8 * D], BF16, actx)
            wo_bf = sb("wo_bf", [128, 8 * D], BF16, actx)
            wkts = sb("wkts", [128, 8 * 16], BF16, actx)
            kbE_s = sb("kbE_s", [16, D], BF16, actx)
            xs = [sb(f"xs{i}", [128, NPT * D], F32, actx) for i in range(3)]
            xs_bf = [sb(f"xsb{i}", [128, NPT * D], BF16, actx) for i in range(2)]
            xT = sb("xT", [128, NDT * NT], F32, actx)
            xTb = [sb(f"xTb{i}", [128, NDT * NT], BF16, actx) for i in range(2)]
            u_t = sb("u_t", [128, NDT * NT], F32, actx)
            w_t = sb("w_t", [128, NDT * NT], F32, actx)
            r_bf = [sb(f"r_bf{i}", [128, NDT * NT], BF16, actx) for i in range(2)]
            s_t = [sb(f"s_t{i}", [128, NDT * NT], F32, actx) for i in range(2)]
            out_f = sb("out_f", [128, NDT * NT], F32, actx)
            sq_t = sb("sq_t", [128, NDT * NT], BF16, actx)
            nrm_bf = sb("nrm_bf", [128, NDT * NT], BF16, actx)
            h_t = [sb(f"h_t{i}", [128, NDT * NT], F32, actx) for i in range(2)]
            hbfs = [sb(f"hbfs{i}", [128, NDT * NT], BF16, actx) for i in range(2)]
            kb_sb = [sb(f"kb_sb{i}", [128, NT], F32, actx) for i in range(2)]
            ksum_sb = sb("ksum_sb", [16, NT], BF16, actx)
            srow = sb("srow", [1, NT * 4], F32, actx)
            lgrow = [sb(f"lgrow{i}", [8, NT], F32, actx) for i in range(2)]
            ltok = sb("ltok", [128, 8], F32, actx)
            gtok = sb("gtok", [128, 8], F32, actx)
            gfm_t = sb("gfm_t", [8, NT], F32, actx)
            gcol = sb("gcol", [128, 8], F32, actx)
            ngsum = sb("ngsum", [8, 1], F32, actx)
            ngs8 = sb("ngs8", [128, 8], F32, actx)
            lt10 = sb("lt10", [128, 16], F32, actx)
            gtok_r = [sb(f"gtokr{i}", [128, 8 * NG], F32, actx) for i in range(2)]
            rrms_sb = sb("rrms_sb", [128, NT], F32, actx)

            pbank = [ps(f"pb{i}", actx) for i in range(3)]
            ptr = ps("ptr", actx)
            pks = actx.enter_context(nc.psum_tensor("pks", [128, 1024], BF16))
            pks2 = actx.enter_context(nc.psum_tensor("pks2", [128, 1024], BF16))
            pst = ps("pst", actx)
            pbc = ps("pbc", actx)

            bank_seq = [0]

            def next_bank():
                b = pbank[bank_seq[0] % 3]
                key = f"pbank{bank_seq[0] % 3}"
                bank_seq[0] += 1
                return b, key

            for nm, dst, srcw in (("wv", wv_bf, WvT), ("ww", ww_bf, WwT),
                                  ("wr", wr_bf, WrT), ("wo", wo_bf, WoT)):
                P.dma("gpsimd", lambda e, dst=dst, srcw=srcw: e.dma_start(
                    out=dst[:].rearrange("p (kt o) -> p kt o", kt=8),
                    in_=srcw.rearrange("(kt kp) o -> kp kt o", kp=128)),
                    writes=[nm])
            P.dma("sync", lambda e: e.dma_start(out=kbE_s[:], in_=kbE), writes=["kbE"])

            for half in range(2):
                P.dma("sync", lambda e, half=half: e.dma_start(
                    out=xs[0][:].rearrange("p (kt o) -> p kt o", kt=2),
                    in_=WkT.rearrange("(h kt kp) o -> h kp kt o", h=2, kp=128)[half][:, 0:2]),
                    writes=[("xs", 0)])
                P.dma("sync", lambda e, half=half: e.dma_start(
                    out=xs[1][:].rearrange("p (kt o) -> p kt o", kt=2),
                    in_=WkT.rearrange("(h kt kp) o -> h kp kt o", h=2, kp=128)[half][:, 2:4]),
                    writes=[("xs", 1)])
                for kt2 in range(4):
                    t = xs[kt2 // 2]
                    off = (kt2 % 2) * D
                    kt = half * 4 + kt2
                    for h in range(16):
                        def _rs(e, t=t, off=off, kt=kt, h=h):
                            with nc.allow_low_precision(reason="bf16 matmul operand (head-summed Wk)"):
                                return e.reduce_sum(
                                    wkts[:, kt * 16 + h: kt * 16 + h + 1],
                                    t[:, off + h * 64: off + (h + 1) * 64], axis=AX.X)
                        P.op("vector", _rs, reads=[("xs", kt2 // 2)], writes=["wkts"])

            for kt in range(8):
                P.op("tensor", lambda e, kt=kt: e.matmul(
                    pst[0:1, 0:8], ones_col[:], gwts[:, kt * 8:(kt + 1) * 8],
                    start=(kt == 0), stop=(kt == 7)),
                    reads=["gwts", "ones_col"], writes=["pst"])
            P.op("scalar", lambda e: e.copy(srow[0:1, 0:8], pst[0:1, 0:8]),
                 reads=["pst"], writes=["srow_gs"])
            P.op("tensor", lambda e: e.transpose(
                pbc[0:8, 0:1], srow[0:1, 0:8], idn[0:1, 0:1]),
                reads=["srow_gs", "idn"], writes=["pbc"])
            P.op("scalar", lambda e: e.mul(ngsum[:], pbc[0:8, 0:1], -1.0),
                 reads=["pbc"], writes=["ngsum"])
            P.op("scalar", lambda e: e.mul(srow[0:1, 8:16], srow[0:1, 0:8], -1.0),
                 reads=["srow_gs"], writes=["srow_ngs"])
            P.op("tensor", lambda e: e.matmul(
                pbc[:, 0:8], ones_row[:], srow[0:1, 8:16], start=True, stop=True),
                reads=["srow_ngs", "ones_row"], writes=["pbc"])
            P.op("vector", lambda e: e.tensor_copy(ngs8[:], pbc[:, 0:8]),
                 reads=["pbc"], writes=["ngs8"])

            def attn_frontA(i, nt_idx, is_pref, xsrc):
                sl = i % 2
                P.dma("gpsimd", lambda e: e.dma_start(
                    out=xs_bf[sl][:].rearrange("p (pt d) -> p pt d", pt=NPT),
                    in_=xsrc.rearrange("(n pt p) d -> n p pt d", p=128, pt=NPT)[nt_idx]),
                    writes=[("xsb", sl)])
                if not is_pref:
                    P.dma("sync", lambda e: e.dma_start(
                        out=xs[i % 3][:].rearrange("p (pt d) -> p pt d", pt=NPT),
                        in_=xsrc.rearrange("(n pt p) d -> n p pt d", p=128, pt=NPT)[nt_idx]),
                        writes=[("xs", i % 3)])
                for pt in range(NPT):
                    tb, tbkey = (pks, "pks") if pt % 2 == 0 else (pks2, "pks2")
                    for dt in range(NDT):
                        P.op("tensor", lambda e, pt=pt, dt=dt, tb=tb, sl=sl: e.transpose(
                            tb[:, dt * 128:(dt + 1) * 128],
                            xs_bf[sl][:, pt * D + dt * 128: pt * D + (dt + 1) * 128],
                            idnb[:]),
                            reads=[("xsb", sl), "idnb"], writes=[tbkey])
                    P.op("vector", lambda e, pt=pt, tb=tb, sl=sl: e.tensor_copy(
                        xTb[sl].rearrange("p (dt t) -> p dt t", dt=NDT)
                        [:, :, pt * 128:(pt + 1) * 128],
                        tb[:].rearrange("p (dt t) -> p dt t", dt=NDT)),
                        reads=[tbkey], writes=[("xTb", sl)])

            def attn_frontB(i, nt_idx, is_pref, tot):
                sl = i % 2
                for kt in range(NDT):
                    P.op("tensor", lambda e, kt=kt: e.matmul(
                        ptr[0:16, 0:NT], wkts[:, kt * 16:(kt + 1) * 16],
                        xTb[sl][:, kt * NT:(kt + 1) * NT],
                        start=(kt == 0), stop=(kt == NDT - 1)),
                        reads=["wkts", ("xTb", sl)],
                        writes=["ptr"])
                P.op("scalar", lambda e: e.copy(ksum_sb[:], ptr[0:16, 0:NT]),
                     reads=["ptr"], writes=["ksum_sb"])

                for dt in range(NDT):
                    bk, bkey = next_bank()
                    for kt in range(NDT):
                        P.op("tensor", lambda e, dt=dt, kt=kt, bk=bk: e.matmul(
                            bk[:, 0:NT], wv_bf[:, kt * D + dt * 128: kt * D + (dt + 1) * 128],
                            xTb[sl][:, kt * NT:(kt + 1) * NT],
                            start=(kt == 0), stop=(kt == NDT - 1)),
                            reads=["wv", ("xTb", sl)], writes=[bkey])
                    P.op("tensor", lambda e, dt=dt: e.matmul(
                        ptr[:, 0:NT], kbE_s[:, dt * 128:(dt + 1) * 128],
                        ksum_sb[:], start=True, stop=True),
                        reads=["kbE", "ksum_sb"], writes=["ptr"])
                    P.op("scalar", lambda e, dt=dt: e.copy(
                        kb_sb[dt % 2][:], ptr[:, 0:NT]),
                        reads=["ptr"], writes=[("kb", dt % 2)])
                    P.op("vector", lambda e, dt=dt, bk=bk: e.tensor_mul(
                        u_t[:, dt * NT:(dt + 1) * NT], bk[:, 0:NT], kb_sb[dt % 2][:]),
                        reads=[bkey, ("kb", dt % 2)], writes=[("u", dt)])
                for dt in range(NDT):
                    bk, bkey = next_bank()
                    for kt in range(NDT):
                        P.op("tensor", lambda e, dt=dt, kt=kt, bk=bk: e.matmul(
                            bk[:, 0:NT], ww_bf[:, kt * D + dt * 128: kt * D + (dt + 1) * 128],
                            xTb[sl][:, kt * NT:(kt + 1) * NT],
                            start=(kt == 0), stop=(kt == NDT - 1)),
                            reads=["ww", ("xTb", sl)], writes=[bkey])
                    P.op("scalar", lambda e, dt=dt, bk=bk: e.activation(
                        w_t[:, dt * NT:(dt + 1) * NT], bk[:, 0:NT], AF.Sigmoid),
                        reads=[bkey], writes=[("wsg", dt)])
                    P.op("vector", lambda e, dt=dt: e.tensor_scalar_mul(
                        w_t[:, dt * NT:(dt + 1) * NT],
                        w_t[:, dt * NT:(dt + 1) * NT], 0.99),
                        reads=[("wsg", dt)], writes=[("w", dt)])
                if not is_pref:
                    for dt in range(NDT):
                        bk, bkey = next_bank()
                        for kt in range(NDT):
                            P.op("tensor", lambda e, dt=dt, kt=kt, bk=bk: e.matmul(
                                bk[:, 0:NT], wr_bf[:, kt * D + dt * 128: kt * D + (dt + 1) * 128],
                                xTb[sl][:, kt * NT:(kt + 1) * NT],
                                start=(kt == 0), stop=(kt == NDT - 1)),
                                reads=["wr", ("xTb", sl)], writes=[bkey])
                        P.op("scalar", lambda e, dt=dt, bk=bk: e.activation(
                            r_bf[sl][:, dt * NT:(dt + 1) * NT], bk[:, 0:NT], AF.Sigmoid),
                            reads=[bkey], writes=[("r", sl, dt)])

                first_main = (not is_pref) and nt_idx == 0
                for dt in range(NDT):
                    if tot == 0:
                        init = 0.0
                        init_reads = []
                    elif first_main:
                        init = minit[:, dt:dt + 1]
                        init_reads = [("minit",)]
                    else:
                        psl = (i - 1) % 2
                        init = s_t[psl][:, dt * NT + NT - 1: dt * NT + NT]
                        init_reads = [("s", psl, dt)]
                    P.op("vector", lambda e, dt=dt, init=init: e.tensor_tensor_scan(
                        s_t[sl][:, dt * NT:(dt + 1) * NT],
                        w_t[:, dt * NT:(dt + 1) * NT],
                        u_t[:, dt * NT:(dt + 1) * NT],
                        init, ALU.mult, ALU.add),
                        reads=[("w", dt), ("u", dt)] + init_reads,
                        writes=[("s", sl, dt)])

                if is_pref and nt_idx == nt_pref - 1:
                    P.op("vector", lambda e: e.tensor_scalar(
                        minit[:],
                        s_t[sl].rearrange("p (dt t) -> p dt t", t=NT)[:, :, NT - 1:NT]
                        .rearrange("p dt one -> p (dt one)"),
                        mask_s[:], None, ALU.mult),
                        reads=[("s", sl, dt2) for dt2 in range(NDT)] + ["mask"],
                        writes=["minit"])

            def attn_back(i, nt_idx):
                sl = i % 2
                t0 = nt_idx * NT
                for pt in range(NPT):
                    tb, tbkey = (pst, "pst") if pt % 2 == 0 else (pbc, "pbc")
                    for dt4 in range(NDT // 4):
                        for j in range(4):
                            dt = dt4 * 4 + j
                            P.op("tensor", lambda e, pt=pt, dt=dt, j=j, tb=tb, i=i: e.transpose(
                                tb[:, j * 128:(j + 1) * 128],
                                xs[i % 3][:, pt * D + dt * 128: pt * D + (dt + 1) * 128],
                                idn[:]),
                                reads=[("xs", i % 3), "idn"], writes=[tbkey])
                        P.op("vector", lambda e, pt=pt, dt4=dt4, tb=tb: e.tensor_copy(
                            xT.rearrange("p (dt t) -> p dt t", dt=NDT)
                            [:, dt4 * 4:(dt4 + 1) * 4, pt * 128:(pt + 1) * 128],
                            tb[:, 0:512].rearrange("p (j t) -> p j t", j=4)),
                            reads=[tbkey], writes=[("xT", dt4)])
                xT_reads = [("xT", dt4) for dt4 in range(NDT // 4)]
                for dt in range(NDT):
                    P.op("vector", lambda e, dt=dt: e.tensor_mul(
                        out_f[:, dt * NT:(dt + 1) * NT],
                        s_t[sl][:, dt * NT:(dt + 1) * NT],
                        r_bf[sl][:, dt * NT:(dt + 1) * NT]),
                        reads=[("s", sl, dt), ("r", sl, dt)], writes=[("outf", dt)])
                    P.op("scalar", lambda e, dt=dt: e.activation(
                        sq_t[:, dt * NT:(dt + 1) * NT],
                        out_f[:, dt * NT:(dt + 1) * NT], AF.Square),
                        reads=[("outf", dt)], writes=[("sq", dt)])
                for dt in range(NDT):
                    P.op("tensor", lambda e, dt=dt: e.matmul(
                        pst[0:1, 0:NT], ones_colb[:], sq_t[:, dt * NT:(dt + 1) * NT],
                        start=(dt == 0), stop=(dt == NDT - 1)),
                        reads=[("sq", dt), "ones_colb"], writes=["pst"])
                P.op("scalar", lambda e: e.activation(
                    srow[0:1, 0:NT], pst[0:1, 0:NT], AF.Sqrt,
                    bias=consts[0:1, 0:1], scale=1.0 / D),
                    reads=["pst", "consts"], writes=["srow_rms0"])
                P.op("vector", lambda e: e.reciprocal(
                    srow[0:1, NT:2 * NT], srow[0:1, 0:NT]),
                    reads=["srow_rms0"], writes=["srow_rms1"])
                P.op("scalar", lambda e: e.copy(nrm_bf[:], out_f[:]),
                     reads=[("outf", dt2) for dt2 in range(NDT)], writes=["nrm"])
                wo_banks = []
                for dt in range(NDT):
                    bk, bkey = next_bank()
                    wo_banks.append((bk, bkey))
                    for kt in range(NDT):
                        P.op("tensor", lambda e, dt=dt, kt=kt, bk=bk: e.matmul(
                            bk[:, 0:NT], wo_bf[:, kt * D + dt * 128: kt * D + (dt + 1) * 128],
                            nrm_bf[:, kt * NT:(kt + 1) * NT],
                            start=(kt == 0), stop=(kt == NDT - 1)),
                            reads=["wo", "nrm"],
                            writes=[bkey])
                    if dt == 1:
                        P.op("tensor", lambda e: e.matmul(
                            pbc[:, 0:NT], ones_row[:], srow[0:1, NT:2 * NT],
                            start=True, stop=True),
                            reads=["srow_rms1", "ones_row"], writes=["pbc"])
                        P.op("vector", lambda e: e.tensor_copy(
                            rrms_sb[:], pbc[:, 0:NT]),
                            reads=["pbc"], writes=["rrms"])
                    if dt >= 1:
                        pd = dt - 1
                        pbk, pbkey = wo_banks[pd]
                        P.op("vector", lambda e, pd=pd, pbk=pbk: e.tensor_mul(
                            out_f[:, pd * NT:(pd + 1) * NT], pbk[:, 0:NT], rrms_sb[:]),
                            reads=[pbkey, "rrms"], writes=[("outf", pd)])
                        P.op("vector", lambda e, pd=pd: e.tensor_add(
                            out_f[:, pd * NT:(pd + 1) * NT],
                            out_f[:, pd * NT:(pd + 1) * NT],
                            xT[:, pd * NT:(pd + 1) * NT]),
                            reads=[("outf", pd)] + xT_reads, writes=[("outf", pd)])
                        P.op("scalar", lambda e, pd=pd: e.activation(
                            sq_t[:, pd * NT:(pd + 1) * NT],
                            out_f[:, pd * NT:(pd + 1) * NT], AF.Square),
                            reads=[("outf", pd)], writes=[("sq", pd)])
                pd = NDT - 1
                pbk, pbkey = wo_banks[pd]
                P.op("vector", lambda e, pd=pd, pbk=pbk: e.tensor_mul(
                    out_f[:, pd * NT:(pd + 1) * NT], pbk[:, 0:NT], rrms_sb[:]),
                    reads=[pbkey, "rrms"], writes=[("outf", pd)])
                P.op("vector", lambda e, pd=pd: e.tensor_add(
                    out_f[:, pd * NT:(pd + 1) * NT],
                    out_f[:, pd * NT:(pd + 1) * NT],
                    xT[:, pd * NT:(pd + 1) * NT]),
                    reads=[("outf", pd)] + xT_reads, writes=[("outf", pd)])
                P.op("scalar", lambda e, pd=pd: e.activation(
                    sq_t[:, pd * NT:(pd + 1) * NT],
                    out_f[:, pd * NT:(pd + 1) * NT], AF.Square),
                    reads=[("outf", pd)], writes=[("sq", pd)])
                for dt in range(NDT):
                    P.op("tensor", lambda e, dt=dt: e.matmul(
                        pst[0:1, 0:NT], ones_col[:], out_f[:, dt * NT:(dt + 1) * NT],
                        start=(dt == 0), stop=(dt == NDT - 1)),
                        reads=[("outf", dt), "ones_col"], writes=["pst"])
                for dt in range(NDT):
                    P.op("tensor", lambda e, dt=dt: e.matmul(
                        pst[0:1, NT:2 * NT], ones_colb[:], sq_t[:, dt * NT:(dt + 1) * NT],
                        start=(dt == 0), stop=(dt == NDT - 1)),
                        reads=[("sq", dt), "ones_colb"], writes=["pst"])
                for kt in range(NDT):
                    P.op("tensor", lambda e, kt=kt: e.matmul(
                        ptr[0:8, 0:NT], gwts[:, kt * 8:(kt + 1) * 8],
                        out_f[:, kt * NT:(kt + 1) * NT],
                        start=(kt == 0), stop=(kt == NDT - 1)),
                        reads=[("outf", kt), "gwts"], writes=["ptr"])
                P.op("scalar", lambda e: e.mul(srow[0:1, 0:NT], pst[0:1, 0:NT], 1.0 / D),
                     reads=["pst"], writes=["srow_m"])
                P.op("scalar", lambda e: e.activation(
                    srow[0:1, NT:2 * NT], srow[0:1, 0:NT], AF.Square),
                    reads=["srow_m"], writes=["srow_m2"])
                P.op("scalar", lambda e: e.mul(srow[0:1, 2 * NT:3 * NT], pst[0:1, NT:2 * NT], 1.0 / D),
                     reads=["pst"], writes=["srow_q"])
                P.op("scalar", lambda e, sl=sl: e.copy(lgrow[sl][:], ptr[0:8, 0:NT]),
                     reads=["ptr"], writes=[("lgrow", sl)])
                P.op("vector", lambda e: e.tensor_sub(
                    srow[0:1, 2 * NT:3 * NT], srow[0:1, 2 * NT:3 * NT], srow[0:1, NT:2 * NT]),
                    reads=["srow_q", "srow_m2"], writes=["srow_var"])
                P.op("scalar", lambda e: e.activation(
                    srow[0:1, 3 * NT:4 * NT], srow[0:1, 2 * NT:3 * NT], AF.Sqrt,
                    bias=consts[0:1, 1:2]),
                    reads=["srow_var", "consts"], writes=["srow_std"])
                P.op("vector", lambda e: e.reciprocal(
                    srow[0:1, 2 * NT:3 * NT], srow[0:1, 3 * NT:4 * NT]),
                    reads=["srow_std"], writes=["srow_rstd"])
                P.op("tensor", lambda e: e.matmul(
                    pbc[:, 0:NT], ones_row[:], srow[0:1, 0:NT], start=True, stop=True),
                    reads=["srow_m", "ones_row"], writes=["pbc"])
                P.op("tensor", lambda e: e.matmul(
                    pst[:, 0:NT], ones_row[:], srow[0:1, 2 * NT:3 * NT], start=True, stop=True),
                    reads=["srow_rstd", "ones_row", "pst"], writes=["pst"])
                for dt in range(NDT):
                    P.op("vector", lambda e, dt=dt: e.tensor_sub(
                        h_t[sl][:, dt * NT:(dt + 1) * NT],
                        out_f[:, dt * NT:(dt + 1) * NT], pbc[:, 0:NT]),
                        reads=[("outf", dt), "pbc"], writes=[("h", sl, dt)])
                for dt in range(NDT):
                    P.op("vector", lambda e, dt=dt: e.tensor_mul(
                        h_t[sl][:, dt * NT:(dt + 1) * NT],
                        h_t[sl][:, dt * NT:(dt + 1) * NT], pst[:, 0:NT]),
                        reads=[("h", sl, dt), "pst"], writes=[("h", sl, dt)])
                h_reads = [("h", sl, dt2) for dt2 in range(NDT)]
                P.dma("sync", lambda e, t0=t0: e.dma_start(
                    out=hscr.rearrange("(dt kp) T -> kp dt T", kp=128)[:, :, t0:t0 + NT],
                    in_=h_t[sl][:].rearrange("p (dt t) -> p dt t", dt=NDT)),
                    reads=h_reads, writes=[("hscr", nt_idx)])
                P.op("scalar", lambda e: e.copy(hbfs[sl][:], h_t[sl][:]),
                     reads=h_reads, writes=[("hbfs", sl)])
                P.dma("sync", lambda e, t0=t0: e.dma_start(
                    out=hscr_bf.rearrange("(dt kp) T -> kp dt T", kp=128)[:, :, t0:t0 + NT],
                    in_=hbfs[sl][:].rearrange("p (dt t) -> p dt t", dt=NDT)),
                    reads=[("hbfs", sl)], writes=[("hscr_bf", nt_idx)])

                for g in range(NG):
                    P.op("tensor", lambda e, g=g, sl=sl: e.transpose(
                        pbc[:, 0:8], lgrow[sl][:, g * 128:(g + 1) * 128], idn[0:8, 0:8]),
                        reads=[("lgrow", sl), "idn"], writes=["pbc"])
                    P.op("tensor", lambda e, g=g: e.transpose(
                        pbc[:, 8:9], srow[0:1, g * 128:(g + 1) * 128], idn[0:1, 0:1]),
                        reads=["srow_m", "idn"], writes=["pbc"])
                    P.op("tensor", lambda e, g=g: e.transpose(
                        pbc[:, 9:10], srow[0:1, 2 * NT + g * 128: 2 * NT + (g + 1) * 128],
                        idn[0:1, 0:1]),
                        reads=["srow_rstd", "idn"], writes=["pbc"])
                    P.op("vector", lambda e: e.tensor_copy(lt10[:, 0:10], pbc[:, 0:10]),
                         reads=["pbc"], writes=["lt10"])
                    P.op("vector", lambda e: e.scalar_tensor_tensor(
                        ltok[:], ngs8[:], lt10[:, 8:9], lt10[:, 0:8], ALU.mult, ALU.add),
                        reads=["lt10", "ngs8"], writes=["ltok"])
                    P.op("vector", lambda e: e.tensor_scalar(
                        ltok[:], ltok[:], lt10[:, 9:10], None, ALU.mult),
                        reads=["ltok", "lt10"], writes=["ltok"])
                    P.op("vector", lambda e: e.reduce_max(gcol[:, 0:1], ltok[:], axis=AX.X),
                         reads=["ltok"], writes=["g_m1"])
                    P.op("vector", lambda e: e.tensor_scalar_mul(gcol[:, 1:2], gcol[:, 0:1], -1.0),
                         reads=["g_m1"], writes=["g_nm1"])
                    P.op("vector", lambda e: e.tensor_scalar(
                        gtok[:], ltok[:], gcol[:, 0:1], None, ALU.is_ge),
                        reads=["ltok", "g_m1"], writes=["gtok"])
                    P.op("vector", lambda e: e.scalar_tensor_tensor(
                        gtok[:], gtok[:], -1e30, ltok[:], ALU.mult, ALU.add),
                        reads=["gtok", "ltok"], writes=["gtok"])
                    P.op("vector", lambda e: e.reduce_max(gcol[:, 2:3], gtok[:], axis=AX.X),
                         reads=["gtok"], writes=["g_m2"])
                    P.op("scalar", lambda e: e.activation(
                        gtok[:], ltok[:], AF.Exp, bias=gcol[:, 1:2], accum_out=gcol[:, 3:4]),
                        reads=["ltok", "g_nm1"], writes=["gtok", "g_z"])
                    P.op("vector", lambda e: e.reciprocal(gcol[:, 4:5], gcol[:, 3:4]),
                         reads=["g_z"], writes=["g_rz"])
                    P.op("vector", lambda e: e.tensor_scalar(
                        ltok[:], ltok[:], gcol[:, 2:3], None, ALU.is_ge),
                        reads=["ltok", "g_m2"], writes=["ltok"])
                    P.op("vector", lambda e: e.tensor_mul(gtok[:], gtok[:], ltok[:]),
                         reads=["gtok", "ltok"], writes=["gtok"])
                    P.op("vector", lambda e, g=g, sl=sl: e.tensor_scalar(
                        gtok_r[sl][:, g * 8:(g + 1) * 8], gtok[:], gcol[:, 4:5], 0.5,
                        ALU.mult, ALU.mult),
                        reads=["gtok", "g_rz"], writes=[("gtokr", sl, g)])

            def attn_gates(i, nt_idx):
                sl = i % 2
                t0 = nt_idx * NT
                for g in range(NG):
                    P.op("tensor", lambda e, g=g, sl=sl: e.transpose(
                        pbc[0:8, 0:128], gtok_r[sl][:, g * 8:(g + 1) * 8], idn[:]),
                        reads=[("gtokr", sl, g), "idn"], writes=["pbc"])
                    P.op("scalar", lambda e, g=g: e.copy(
                        gfm_t[:, g * 128:(g + 1) * 128], pbc[0:8, 0:128]),
                        reads=["pbc"], writes=[("gfm", g)])
                P.dma("sync", lambda e, t0=t0: e.dma_start(
                    out=gscr[:, t0:t0 + NT], in_=gfm_t[:]),
                    reads=[("gfm", g) for g in range(NG)], writes=[("gscr", nt_idx)])

            tiles = [(ntp, True) for ntp in range(nt_pref)] + \
                    [(ntm, False) for ntm in range(nt_main)]
            stages = []
            for i, (nt_idx, is_pref) in enumerate(tiles):
                stages.append((i, nt_idx, is_pref))
            n_tiles = len(stages)
            for i in range(n_tiles + 3):
                if i < n_tiles:
                    ii, nt_idx, is_pref = stages[i]
                    attn_frontA(ii, nt_idx, is_pref, x_pref if is_pref else x_main)
                if 1 <= i <= n_tiles:
                    ii, nt_idx, is_pref = stages[i - 1]
                    attn_frontB(ii, nt_idx, is_pref, i - 1)
                if 2 <= i <= n_tiles + 1:
                    ii, nt_idx, is_pref = stages[i - 2]
                    if not is_pref:
                        attn_back(ii, nt_idx)
                if i >= 3:
                    ii, nt_idx, is_pref = stages[i - 3]
                    if not is_pref:
                        attn_gates(ii, nt_idx)

        P.barrier()

        with contextlib.ExitStack() as fctx:
            h_bfb = [sb(f"h_bfb{i}", [128, NDT * NB], BF16, fctx) for i in range(2)]
            h_res = [sb(f"h_res{i}", [128, NSUB], F32, fctx) for i in range(2)]
            gu = sb("gu", [128, NFC * NB], BF16, fctx)
            y_acc = sb("y_acc", [128, NDT * NB], F32, fctx)
            ws1 = [sb(f"ws1_{i}", [128, 2048], BF16, fctx) for i in range(4)]
            ws2 = [sb(f"ws2_{i}", [128, 4096], BF16, fctx) for i in range(2)]
            gbc = [sb(f"gbc{i}", [128, NSUB], F32, fctx) for i in range(2 * max(n_ns, 1))]
            sg_sb = [sb(f"sg{i}", [128, NSUB], BF16, fctx) for i in range(2)]
            ug_sb = [sb(f"ug{i}", [128, NSUB], BF16, fctx) for i in range(2)]
            ge_row = sb("ge_row", [1, NB], F32, fctx)
            r2tok = [sb(f"r2tok{i}", [128, D], F32, fctx) for i in range(2)]
            otok = [sb(f"otok{i}", [128, D], F32, fctx) for i in range(2)]
            ccol = sb("ccol", [128, 8], F32, fctx)

            psg = [ps(f"psg{i}", fctx) for i in range(2)]
            psu = [ps(f"psu{i}", fctx) for i in range(2)]
            psy = [ps(f"psy{i}", fctx) for i in range(2)]
            psx = ps("psx", fctx)
            psx2 = ps("psx2", fctx)

            def load_hbfb(nb2):
                b0x = nb2 * NB
                hbx = nb2 % 2
                P.dma("sync", lambda e, b0x=b0x, hbx=hbx: e.dma_start(
                    out=h_bfb[hbx][:].rearrange("p (dt t) -> p dt t", dt=NDT),
                    in_=hscr_bf.rearrange("(dt kp) T -> kp dt T", kp=128)[:, :, b0x:b0x + NB]),
                    reads=[("hscr_bf", q) for q in
                           range(nb2 * (NB // NT), (nb2 + 1) * (NB // NT))],
                    writes=[("h_bfb", hbx)])

            load_hbfb(0)
            for nb in range(n_nb):
                b0 = nb * NB
                hb = nb % 2
                hq = list(range(nb * (NB // NT), (nb + 1) * (NB // NT)))

                def emit_gbc(ee2, b0=b0, hq=hq):
                    P.dma("sync", lambda e, ee2=ee2, b0=b0: e.dma_start(
                        out=ge_row[:], in_=gscr[ee2:ee2 + 1, b0:b0 + NB]),
                        reads=[("gscr", q) for q in hq],
                        writes=["ge_row"])
                    sl2 = ee2 % 2
                    for ns in range(n_ns):
                        P.op("tensor", lambda e, ns=ns: e.matmul(
                            psx2[:, 0:NSUB], ones_row[:],
                            ge_row[0:1, ns * NSUB:(ns + 1) * NSUB],
                            start=True, stop=True),
                            reads=["ge_row", "ones_row"], writes=["psx2"])
                        P.op("vector", lambda e, ns=ns, sl2=sl2: e.tensor_copy(
                            gbc[sl2 * n_ns + ns][:], psx2[:, 0:NSUB]),
                            reads=["psx2"], writes=[("gbc", sl2, ns)])

                if nb == 0:
                    emit_gbc(0)
                for ee in range(NEXP):
                    for fc in range(NFC):
                        ci = ee * NFC + fc
                        slw = ci % 4
                        if fc == 2 and ee + 1 < 8:
                            emit_gbc(ee + 1)
                        if fc == 4 and ee == 4 and nb + 1 < n_nb:
                            load_hbfb(nb + 1)
                        if fc == 2 and ee == 8 and nb + 1 < n_nb:
                            emit_gbc(0, b0=(nb + 1) * NB,
                                     hq=list(range((nb + 1) * (NB // NT),
                                                   (nb + 2) * (NB // NT))))
                        if nb == 0:
                            P.dma("gpsimd", lambda e, ci=ci, slw=slw: e.dma_start(
                                out=ws1[slw][:], in_=Wst1[ci]),
                                writes=[("ws1", slw)])
                            P.dma("sync", lambda e, ci=ci, slw=slw: e.dma_start(
                                out=wbs1[ci], in_=ws1[slw][:]),
                                reads=[("ws1", slw)], writes=[("wbs1", ci)])
                        else:
                            P.dma("sync", lambda e, ci=ci, slw=slw: e.dma_start(
                                out=ws1[slw][:], in_=wbs1[ci]),
                                reads=[("wbs1", ci)], writes=[("ws1", slw)])
                        for ns in range(n_ns):
                            pgi = psg[ns % 2]
                            pui = psu[ns % 2]
                            for kt in range(NDT):
                                P.op("tensor", lambda e, ns=ns, kt=kt, slw=slw, pgi=pgi, hb=hb: e.matmul(
                                    pgi[:, 0:NSUB], ws1[slw][:, kt * 128:(kt + 1) * 128],
                                    h_bfb[hb][:, kt * NB + ns * NSUB: kt * NB + (ns + 1) * NSUB],
                                    start=(kt == 0), stop=(kt == NDT - 1)),
                                    reads=[("ws1", slw), ("h_bfb", hb)], writes=[f"psg{ns % 2}"])
                            for kt in range(NDT):
                                P.op("tensor", lambda e, ns=ns, kt=kt, slw=slw, pui=pui, hb=hb: e.matmul(
                                    pui[:, 0:NSUB], ws1[slw][:, 1024 + kt * 128: 1024 + (kt + 1) * 128],
                                    h_bfb[hb][:, kt * NB + ns * NSUB: kt * NB + (ns + 1) * NSUB],
                                    start=(kt == 0), stop=(kt == NDT - 1)),
                                    reads=[("ws1", slw), ("h_bfb", hb)], writes=[f"psu{ns % 2}"])
                            P.op("scalar", lambda e, ns=ns, pgi=pgi: e.activation(
                                sg_sb[ns % 2][:], pgi[:, 0:NSUB], AF.Silu),
                                reads=[f"psg{ns % 2}"], writes=[("sg", ns % 2)])
                            if ee < 8:
                                P.op("vector", lambda e, ns=ns, pui=pui, ee=ee: e.tensor_mul(
                                    ug_sb[ns % 2][:], pui[:, 0:NSUB],
                                    gbc[(ee % 2) * n_ns + ns][:]),
                                    reads=[f"psu{ns % 2}", ("gbc", ee % 2, ns)],
                                    writes=[("ug", ns % 2)])
                            else:
                                P.op("vector", lambda e, ns=ns, pui=pui: e.tensor_scalar(
                                    ug_sb[ns % 2][:], pui[:, 0:NSUB], 0.0625, None, ALU.mult),
                                    reads=[f"psu{ns % 2}"], writes=[("ug", ns % 2)])
                            P.op("vector", lambda e, fc=fc, ns=ns: e.tensor_mul(
                                gu[:, fc * NB + ns * NSUB: fc * NB + (ns + 1) * NSUB],
                                sg_sb[ns % 2][:], ug_sb[ns % 2][:]),
                                reads=[("sg", ns % 2), ("ug", ns % 2)],
                                writes=[("gu", fc, ns)])
                    for dt in range(NDT):
                        di = ee * NDT + dt
                        slw = di % 2
                        if nb == 0:
                            P.dma("gpsimd", lambda e, di=di, slw=slw: e.dma_start(
                                out=ws2[slw][:], in_=Wst2[di]),
                                writes=[("ws2", slw)])
                            P.dma("sync", lambda e, di=di, slw=slw: e.dma_start(
                                out=wbs2[di], in_=ws2[slw][:]),
                                reads=[("ws2", slw)], writes=[("wbs2", di)])
                        else:
                            P.dma("sync", lambda e, di=di, slw=slw: e.dma_start(
                                out=ws2[slw][:], in_=wbs2[di]),
                                reads=[("wbs2", di)], writes=[("ws2", slw)])
                        for ns in range(n_ns):
                            pyi = psy[ns % 2]
                            gu_reads = [("gu", fkt, ns) for fkt in range(NFC)]
                            for fkt in range(NFC):
                                P.op("tensor", lambda e, ns=ns, fkt=fkt, slw=slw, pyi=pyi: e.matmul(
                                    pyi[:, 0:NSUB], ws2[slw][:, fkt * 128:(fkt + 1) * 128],
                                    gu[:, fkt * NB + ns * NSUB: fkt * NB + (ns + 1) * NSUB],
                                    start=(fkt == 0), stop=(fkt == NFC - 1)),
                                    reads=[("ws2", slw)] + (gu_reads if fkt == 0 else [("gu", fkt, ns)]),
                                    writes=[f"psy{ns % 2}"])
                            if ee == 0:
                                P.op("vector", lambda e, dt=dt, ns=ns, pyi=pyi: e.tensor_copy(
                                    y_acc[:, dt * NB + ns * NSUB: dt * NB + (ns + 1) * NSUB],
                                    pyi[:, 0:NSUB]),
                                    reads=[f"psy{ns % 2}"], writes=[("y", dt, ns)])
                            else:
                                P.op("vector", lambda e, dt=dt, ns=ns, pyi=pyi: e.tensor_add(
                                    y_acc[:, dt * NB + ns * NSUB: dt * NB + (ns + 1) * NSUB],
                                    y_acc[:, dt * NB + ns * NSUB: dt * NB + (ns + 1) * NSUB],
                                    pyi[:, 0:NSUB]),
                                    reads=[f"psy{ns % 2}", ("y", dt, ns)],
                                    writes=[("y", dt, ns)])
                            if ee == NEXP - 1:
                                hs = (dt * n_ns + ns) % 2
                                P.dma("sync", lambda e, dt=dt, ns=ns, hs=hs, b0=b0: e.dma_start(
                                    out=h_res[hs][:],
                                    in_=hscr[dt * 128:(dt + 1) * 128,
                                             b0 + ns * NSUB: b0 + (ns + 1) * NSUB]),
                                    reads=[("hscr", q) for q in hq],
                                    writes=[("h_res", hs)])
                                P.op("vector", lambda e, dt=dt, ns=ns, hs=hs: e.tensor_add(
                                    y_acc[:, dt * NB + ns * NSUB: dt * NB + (ns + 1) * NSUB],
                                    y_acc[:, dt * NB + ns * NSUB: dt * NB + (ns + 1) * NSUB],
                                    h_res[hs][:]),
                                    reads=[("y", dt, ns), ("h_res", hs)], writes=[("y", dt, ns)])

                for tg in range(NB // 128):
                    stok = tg % 2
                    yreads = [("y", dt, (tg * 128) // NSUB) for dt in range(NDT)]
                    for half in range(2):
                        px, pxkey = (psx, "psx") if half == 0 else (psx2, "psx2")
                        for j in range(4):
                            dt = half * 4 + j
                            P.op("tensor", lambda e, dt=dt, j=j, tg=tg, px=px: e.transpose(
                                px[:, j * 128:(j + 1) * 128],
                                y_acc[:, dt * NB + tg * 128: dt * NB + (tg + 1) * 128],
                                idn[:]),
                                reads=yreads if (half == 0 and j == 0) else
                                [("y", dt, (tg * 128) // NSUB)],
                                writes=[pxkey])
                        P.op("vector", lambda e, half=half, stok=stok, px=px, pxkey=pxkey: e.tensor_copy(
                            r2tok[stok][:, half * 512:(half + 1) * 512], px[:, 0:512]),
                            reads=[pxkey], writes=[("r2", stok, half)])
                    r2r = [("r2", stok, 0), ("r2", stok, 1)]
                    P.op("vector", lambda e, stok=stok: e.reduce_sum(
                        ccol[:, 0:1], r2tok[stok][:], axis=AX.X),
                        reads=r2r, writes=["cc_s"])
                    P.op("scalar", lambda e, stok=stok: e.activation(
                        otok[stok][:], r2tok[stok][:], AF.Square, accum_out=ccol[:, 1:2]),
                        reads=r2r, writes=[("otok", stok), "cc_q"])
                    P.op("scalar", lambda e: e.mul(ccol[:, 2:3], ccol[:, 0:1], 1.0 / D),
                         reads=["cc_s"], writes=["cc_m"])
                    P.op("scalar", lambda e: e.activation(
                        ccol[:, 3:4], ccol[:, 2:3], AF.Square),
                        reads=["cc_m"], writes=["cc_m2"])
                    P.op("scalar", lambda e: e.mul(ccol[:, 4:5], ccol[:, 1:2], 1.0 / D),
                         reads=["cc_q"], writes=["cc_q2"])
                    P.op("vector", lambda e: e.tensor_sub(
                        ccol[:, 5:6], ccol[:, 4:5], ccol[:, 3:4]),
                        reads=["cc_q2", "cc_m2"], writes=["cc_var"])
                    P.op("scalar", lambda e: e.activation(
                        ccol[:, 6:7], ccol[:, 5:6], AF.Sqrt, bias=consts[:, 1:2]),
                        reads=["cc_var", "consts"], writes=["cc_std"])
                    P.op("vector", lambda e: e.reciprocal(ccol[:, 7:8], ccol[:, 6:7]),
                         reads=["cc_std"], writes=["cc_rstd"])
                    P.op("vector", lambda e, stok=stok: e.tensor_scalar(
                        otok[stok][:], r2tok[stok][:], ccol[:, 2:3], ccol[:, 7:8],
                        ALU.subtract, ALU.mult),
                        reads=r2r + ["cc_m", "cc_rstd"], writes=[("otok", stok)])
                    st = P.dma("gpsimd", lambda e, b0=b0, tg=tg, stok=stok: e.dma_start(
                        out=out_t[b0 + tg * 128: b0 + (tg + 1) * 128, :],
                        in_=otok[stok][:]),
                        reads=[("otok", stok)], writes=[("out", nb, tg)])
                    out_stores.append(st)

        P.finalize_values()
        with nc.Block() as block:
            @block.sync
            def _(e):
                P.emit_engine("sync", e)
                P.final_wait("sync", e, out_stores)

            @block.gpsimd
            def _(e):
                P.emit_engine("gpsimd", e)

            @block.tensor
            def _(e):
                P.emit_engine("tensor", e)

            @block.scalar
            def _(e):
                P.emit_engine("scalar", e)

            @block.vector
            def _(e):
                P.emit_engine("vector", e)

    return nc, "out"


def prep_inputs(inputs, T_main=4096, T_pref=4096):
    x = np.ascontiguousarray(inputs["L0_attn_x"])
    B, T, D = x.shape
    NFC, NDT, NEXP = 32, 8, 9

    def t(a):
        return np.ascontiguousarray(np.asarray(a).T)

    WrT, WkT, WvT, WwT, WoT = (t(inputs["L0_attn_Wr"]), t(inputs["L0_attn_Wk"]),
                               t(inputs["L0_attn_Wv"]), t(inputs["L0_attn_Ww"]),
                               t(inputs["L0_attn_Wo"]))
    gWT = t(inputs["L0_ffn_gate_W"])

    w1 = [np.asarray(inputs["L0_ffn_w1_e"][e]) for e in range(8)]
    w3 = [np.asarray(inputs["L0_ffn_w3_e"][e]) for e in range(8)]
    w2 = [np.asarray(inputs["L0_ffn_w2_e"][e]) for e in range(8)]
    w1.append(np.concatenate([inputs["L0_ffn_w1_s"][n] for n in range(8)], axis=1))
    w3.append(np.concatenate([inputs["L0_ffn_w3_s"][n] for n in range(8)], axis=1))
    w2.append(np.concatenate([inputs["L0_ffn_w2_s"][n] for n in range(8)], axis=0))

    st1 = np.empty((NEXP * NFC, 128, 2048), np.float32)
    st2 = np.empty((NEXP * NDT, 128, 4096), np.float32)
    for e in range(NEXP):
        a1 = w1[e].reshape(8, 128, NFC, 128).transpose(2, 1, 0, 3).reshape(NFC, 128, 1024)
        a3 = w3[e].reshape(8, 128, NFC, 128).transpose(2, 1, 0, 3).reshape(NFC, 128, 1024)
        st1[e * NFC:(e + 1) * NFC] = np.concatenate([a1, a3], axis=2)
        st2[e * NDT:(e + 1) * NDT] = (w2[e].reshape(NFC, 128, NDT, 128)
                                      .transpose(2, 1, 0, 3).reshape(NDT, 128, 4096))

    import ml_dtypes
    kbE = np.zeros((16, D), ml_dtypes.bfloat16)
    for c in range(D):
        kbE[(c % 128 >= 64) + 2 * (c // 128), c] = 1.0

    common = {
        "WrT": WrT, "WkT": WkT, "WvT": WvT, "WwT": WwT, "WoT": WoT,
        "gWT": np.ascontiguousarray(gWT), "Wst1": st1, "Wst2": st2, "kbE": kbE,
        "idn": np.eye(128, dtype=np.float32),
        "idnb": np.eye(128, dtype=ml_dtypes.bfloat16),
        "ones_col": np.ones((128, 1), np.float32),
        "ones_colb": np.ones((128, 1), ml_dtypes.bfloat16),
        "ones_row": np.ones((1, 128), np.float32),
    }
    n_half = T // T_main
    in_maps = []
    for core in range(B * n_half):
        b = core // n_half
        half = core % n_half
        m = dict(common)
        m["x_main"] = np.ascontiguousarray(x[b, half * T_main:(half + 1) * T_main])
        m["x_pref"] = np.ascontiguousarray(x[b, 0:max(T_pref, 1)])
        m["mask"] = np.full((128, 1), float(half > 0), np.float32)
        in_maps.append(m)
    return in_maps


def kernel(**inputs):
    T_main, T_pref = 4096, 4096
    x = inputs["L0_attn_x"]
    B, T, D = x.shape
    nc, out_name = build_megakernel(T_main=T_main, T_pref=T_pref)
    in_maps = prep_inputs(inputs, T_main=T_main, T_pref=T_pref)
    res = run_bass_kernel_spmd(nc, in_maps, core_ids=list(range(8)))
    out = np.empty((B, T, D), np.float32)
    n_half = T // T_main
    for core in range(8):
        b, half = core // n_half, core % n_half
        out[b, half * T_main:(half + 1) * T_main] = res.results[core][out_name]
    return out
